# version 07
# baseline (speedup 1.0000x reference)
from contextlib import ExitStack

import numpy as np
import ml_dtypes
import concourse.bass as bass
import concourse.mybir as mybir
from concourse.bass_utils import run_bass_kernel_spmd

F32 = mybir.dt.float32
BF16 = mybir.dt.bfloat16
AF = mybir.ActivationFunctionType
OP = mybir.AluOpType
BF = ml_dtypes.bfloat16

B, S, D = 2, 8192, 1024
NCORES = 8
RANKS = 4
T = S // RANKS
DCH = D // 128
NH0, HD0 = 16, 64
DQ = DKV = 512
SCALE0 = 1.0 / float(np.sqrt(np.float32(NH0 + DKV + DQ)))
LN_EPS = 1e-5
RMS_EPS = 1e-6
TB = 512
NTB = T // TB


class Sems:

    def __init__(self, nc, ctx):
        self.nc, self.ctx = nc, ctx
        self.h = {}

    def __getitem__(self, name):
        if name not in self.h:
            self.h[name] = self.ctx.enter_context(self.nc.semaphore(name))
        return self.h[name]


class PSems:

    def __init__(self, pool, base=None, pre="", dedicated=()):
        self.pool = pool
        self.base, self.pre, self.ded = base, pre, set(dedicated)
        self.map = {}

    def __getitem__(self, k):
        if k in self.ded:
            return self.base[self.pre + k]
        if k not in self.map:
            assert len(self.map) < len(self.pool), "semaphore pool exhausted"
            self.map[k] = self.pool[len(self.map)]
        return self.map[k]


def clear_pool(nc, pool):
    with nc.Block() as blk:
        @blk.sync
        def _(s):
            for h in pool:
                s.sem_clear(h)


def sb(nc, ctx, name, shape, dtype):
    pre = getattr(nc, "_name_pre", "")
    return ctx.enter_context(nc.sbuf_tensor(pre + name, shape, dtype))


def ps(nc, ctx, name, shape, dtype=F32):
    pre = getattr(nc, "_name_pre", "")
    return ctx.enter_context(nc.psum_tensor(pre + name, shape, dtype))


def emit_ln(nc, sm, g, xsrc_ap, ncols, xn_tile, xn_stride, tag, wait_first=None):
    blocks = [(c0, min(512, ncols - c0)) for c0 in range(0, ncols, 512)]
    if len(blocks) > 1 and blocks[-1][1] < 512:
        blocks[-1] = (ncols - 512, 512)
    nb = len(blocks)
    xb, sq = g["ln_xbuf"], g["ln_sq"]
    p_s1, p_s2, p_a, p_b = g["p_a1"], g["p_a2"], g["p_b1"], g["p_b2"]
    rows = g["ln_rows"]
    abc, bbc = g["ln_abc"], g["ln_bbc"]
    ones128, ones1r = g["ones128"], g["ones1r"]

    def R(k, w):
        return rows[0:1, 512 * k: 512 * k + w]

    with nc.Block() as blk:
        @blk.sync
        def _(s):
            if wait_first is not None:
                s.wait_ge(sm[wait_first[0]], wait_first[1])
            for i, (c0, w) in enumerate(blocks):
                if i >= 2:
                    s.wait_ge(sm[tag + "xn"], i - 1)
                for ch in range(DCH):
                    s.dma_start(xb[i % 2][:, ch * 512: ch * 512 + w],
                                xsrc_ap(ch, c0, w)).then_inc(sm[tag + "xld"], 16)

        @blk.scalar
        def _(sc):
            for i, (c0, w) in enumerate(blocks):
                sc.wait_ge(sm[tag + "xld"], 16 * DCH * (i + 1))
                for ch in range(DCH):
                    if i > 0 or ch > 0:
                        sc.wait_ge(sm[tag + "smm"], i * DCH + ch)
                    sc.activation(sq[:, 0:w], xb[i % 2][:, ch * 512: ch * 512 + w],
                                  AF.Square).then_inc(sm[tag + "sq"], 1)
                sc.wait_ge(sm[tag + "lnv1"], i + 1)
                sc.activation(R(3, w), R(2, w), AF.Ln, bias=g["cc"][0:1, 0:1])
                sc.activation(R(4, w), R(3, w), AF.Exp, scale=-0.5).then_inc(sm[tag + "rstd"], 1)
                sc.wait_ge(sm[tag + "bcmm"], i + 1)
                if i > 0:
                    sc.wait_ge(sm[tag + "xn"], i)
                sc.copy(abc[:, 0:w], p_a[:, 0:w])
                sc.copy(bbc[:, 0:w], p_b[:, 0:w]).then_inc(sm[tag + "bcev"], 1)

        @blk.tensor
        def _(te):
            for i, (c0, w) in enumerate(blocks):
                for ch in range(DCH):
                    te.wait_ge(sm[tag + "sq"], i * DCH + ch + 1)
                    if ch == 0 and i > 0:
                        te.wait_ge(sm[tag + "lnv1"], i)
                    te.matmul(p_s2[0:1, 0:w], ones128[:], sq[:, 0:w],
                              start=(ch == 0), stop=(ch == DCH - 1))
                    te.matmul(p_s1[0:1, 0:w], ones128[:],
                              xb[i % 2][:, ch * 512: ch * 512 + w],
                              start=(ch == 0), stop=(ch == DCH - 1)
                              ).then_inc(sm[tag + "smm"], 1)
                te.wait_ge(sm[tag + "lnv2"], i + 1)
                if i > 0:
                    te.wait_ge(sm[tag + "bcev"], i)
                te.matmul(p_a[:, 0:w], ones1r[:], R(4, w), start=True, stop=True)
                te.matmul(p_b[:, 0:w], ones1r[:], R(5, w), start=True, stop=True
                          ).then_inc(sm[tag + "bcmm"], 1)

        @blk.vector
        def _(v):
            for i, (c0, w) in enumerate(blocks):
                v.wait_ge(sm[tag + "smm"], (i + 1) * DCH)
                if i >= 1:
                    v.wait_ge(sm[tag + "rstd"], i)
                v.tensor_scalar(R(0, w), p_s1[0:1, 0:w], 1.0 / D, None, op0=OP.mult)
                v.tensor_tensor(R(1, w), R(0, w), R(0, w), op=OP.mult)
                v.scalar_tensor_tensor(R(2, w), p_s2[0:1, 0:w], 1.0 / D, R(1, w),
                                       op0=OP.mult, op1=OP.subtract
                                       ).then_inc(sm[tag + "lnv1"], 1)
                v.wait_ge(sm[tag + "rstd"], i + 1)
                v.scalar_tensor_tensor(R(5, w), R(0, w), -1.0, R(4, w),
                                       op0=OP.mult, op1=OP.mult
                                       ).then_inc(sm[tag + "lnv2"], 1)
                v.wait_ge(sm[tag + "bcev"], i + 1)
                for ch in range(DCH):
                    v.tensor_tensor(g["ln_t"][:, 0:w],
                                    xb[i % 2][:, ch * 512: ch * 512 + w],
                                    abc[:, 0:w], op=OP.mult)
                    dst = xn_tile[:, ch * xn_stride + c0: ch * xn_stride + c0 + w]
                    if w >= 8:
                        ins = v.tensor_tensor(dst, g["ln_t"][:, 0:w], bbc[:, 0:w],
                                              op=OP.add)
                    else:
                        v.tensor_tensor(g["nw"][:, 0:w], g["ln_t"][:, 0:w],
                                        bbc[:, 0:w], op=OP.add)
                        ins = v.tensor_copy(dst, g["nw"][:, 0:w])
                ins.then_inc(sm[tag + "xn"], 1)
    return nb


def emit_wload(nc, sm, g, blk, wap, nkc, ncols, dstslot, tag, after=None):
    stage = g["wstage"]

    @blk.sync
    def _(s):
        for kc in range(nkc):
            if kc == 0 and after is not None:
                s.wait_ge(sm[after[0]], after[1])
            if kc >= 2:
                s.wait_ge(sm[tag + "wc"], kc - 1)
            s.dma_start(stage[kc % 2][:, 0:ncols],
                        wap[kc * 128:(kc + 1) * 128, 0:ncols]
                        ).then_inc(sm[tag + "wld"], 16)

    @blk.gpsimd
    def _(gp):
        for kc in range(nkc):
            gp.wait_ge(sm[tag + "wld"], 16 * (kc + 1))
            gp.tensor_copy(g["wbf"][:, dstslot + kc * ncols: dstslot + (kc + 1) * ncols],
                           stage[kc % 2][:, 0:ncols]).then_inc(sm[tag + "wc"], 1)


def emit_proj_plain(nc, sm, g, wap, nkc, nmc, src_ap, dst_ap, bias_col, tag,
                    out_f32=False, scale=None):
    pp = [g["p_a1"], g["p_a2"]]
    with nc.Block() as blk:
        emit_wload(nc, sm, g, blk, wap, nkc, nmc * 128, 0, tag)

        @blk.tensor
        def _(te):
            te.wait_ge(sm[tag + "wc"], nkc)
            for m in range(nmc):
                for tb in range(NTB):
                    i = m * NTB + tb
                    if i >= 2:
                        te.wait_ge(sm[tag + "ev"], i - 1)
                    for kc in range(nkc):
                        ins = te.matmul(pp[i % 2][:, 0:TB],
                                        g["wbf"][:, kc * nmc * 128 + m * 128:
                                                 kc * nmc * 128 + m * 128 + 128],
                                        src_ap(kc, tb * TB, TB),
                                        start=(kc == 0), stop=(kc == nkc - 1))
                    ins.then_inc(sm[tag + "mm"], 1)

        @blk.scalar
        def _(sc):
            for m in range(nmc):
                for tb in range(NTB):
                    i = m * NTB + tb
                    sc.wait_ge(sm[tag + "mm"], i + 1)
                    kw = {}
                    if bias_col is not None:
                        kw["bias"] = bias_col(m)
                    if scale is not None:
                        kw["scale"] = scale
                    sc.activation(dst_ap(m, tb * TB, TB), pp[i % 2][:, 0:TB],
                                  AF.Identity, **kw).then_inc(sm[tag + "ev"], 1)


def emit_rope_proj(nc, sm, g, wr_ap, nkc_r, wc_ap, nkc_c, src_r_ap, src_c_ap,
                   br_row_ap, bc_col, ctabT_ap, stabT_ap, out_dram, tag):
    ppr = [g["p_a1"], g["p_a2"]]
    ppc = [g["p_b1"], g["p_b2"]]
    tpp = g["tpp"]
    ctb, stb = g["ctb"], g["stb"]
    evtm, roped = g["evtm"], g["roped"]
    evc, outst = g["evac"], g["outst"]
    bbias = g["bbias"]
    with nc.Block() as blk:
        emit_wload(nc, sm, g, blk, wr_ap, nkc_r, 1024, 0, tag + "r")
        emit_wload(nc, sm, g, blk, wc_ap, nkc_c, 1024, 8 * 1024, tag + "c",
                   after=(tag + "rwc", nkc_r))

        @blk.sync
        def _(s):
            s.dma_start(g["bvv"][0:1, :], br_row_ap).then_inc(sm[tag + "brow"], 16)
            for t in range(16):
                if t >= 2:
                    s.wait_ge(sm[tag + "rtile"], t - 1)
                s.dma_start(ctb[t % 2][:, 0:512],
                            ctabT_ap[t * 128:(t + 1) * 128, :]).then_inc(sm[tag + "tab"], 16)
                s.dma_start(stb[t % 2][:, 0:512],
                            stabT_ap[t * 128:(t + 1) * 128, :]).then_inc(sm[tag + "tab"], 16)

        @blk.tensor
        def _(te):
            te.wait_ge(sm[tag + "brow"], 16)
            te.matmul(ppc[0][:, 0:512], g["ones1r"][:], g["bvv"][0:1, 0:512],
                      start=True, stop=True)
            te.matmul(ppc[1][:, 0:512], g["ones1r"][:], g["bvv"][0:1, 512:1024],
                      start=True, stop=True).then_inc(sm[tag + "bcm"], 1)
            te.wait_ge(sm[tag + "rwc"], nkc_r)
            for t in range(16):
                for hf in range(2):
                    i = t * 2 + hf
                    if i >= 2:
                        te.wait_ge(sm[tag + "badd"], i - 1)
                    for kc in range(nkc_r):
                        ins = te.matmul(ppr[i % 2][:, 0:512],
                                        src_r_ap(kc, t * 128, 128),
                                        g["wbf"][:, kc * 1024 + hf * 512: kc * 1024 + hf * 512 + 512],
                                        start=(kc == 0), stop=(kc == nkc_r - 1))
                    ins.then_inc(sm[tag + "mmr"], 1)
            te.wait_ge(sm[tag + "cwc"], nkc_c)
            for m in range(DCH):
                for tb in range(NTB):
                    i = m * NTB + tb
                    if i >= 2:
                        te.wait_ge(sm[tag + "evc"], i - 1)
                    for kc in range(nkc_c):
                        ins = te.matmul(ppc[i % 2][:, 0:512],
                                        g["wbf"][:, 8 * 1024 + kc * 1024 + m * 128:
                                                 8 * 1024 + kc * 1024 + m * 128 + 128],
                                        src_c_ap(kc, tb * TB, TB),
                                        start=(kc == 0), stop=(kc == nkc_c - 1))
                    ins.then_inc(sm[tag + "mmc"], 1)
                    for ts in range(4):
                        j = i * 4 + ts
                        tt = tb * 4 + ts
                        te.wait_ge(sm[tag + "rtile"], tt + 1)
                        if j >= 2:
                            te.wait_ge(sm[tag + "addt"], j - 1)
                        te.transpose(tpp[j % 2][:],
                                     roped[:, tt * 1024 + m * 128: tt * 1024 + m * 128 + 128],
                                     g["identb"][:]).then_inc(sm[tag + "mmt"], 1)

        @blk.scalar
        def _(sc):
            sc.wait_ge(sm[tag + "bcm"], 1)
            sc.copy(bbias[:, 0:512], ppc[0][:, 0:512])
            sc.copy(bbias[:, 512:1024], ppc[1][:, 0:512]).then_inc(sm[tag + "bce"], 1)
            for m in range(DCH):
                for tb in range(NTB):
                    i = m * NTB + tb
                    sc.wait_ge(sm[tag + "mmc"], i + 1)
                    if i > 0:
                        sc.wait_ge(sm[tag + "addt"], 4 * i)
                    sc.activation(evc[:, 0:512], ppc[i % 2][:, 0:512], AF.Identity,
                                  bias=bc_col(m)).then_inc(sm[tag + "evc"], 1)

        @blk.vector
        def _(v):
            v.wait_ge(sm[tag + "bce"], 1)
            for t in range(16):
                for hf in range(2):
                    i = t * 2 + hf
                    v.wait_ge(sm[tag + "mmr"], i + 1)
                    v.tensor_tensor(evtm[:, hf * 512: hf * 512 + 512],
                                    ppr[i % 2][:, 0:512], bbias[:, hf * 512: hf * 512 + 512],
                                    op=OP.add).then_inc(sm[tag + "badd"], 1)
                v.wait_ge(sm[tag + "tab"], 32 * (t + 1))
                for hf in range(2):
                    c0 = t * 1024 + hf * 512
                    ev = evtm[:, hf * 512: hf * 512 + 512]
                    even, odd = ev[:, 0:512:2], ev[:, 1:512:2]
                    cts = ctb[t % 2][:, hf * 256: hf * 256 + 256]
                    sts = stb[t % 2][:, hf * 256: hf * 256 + 256]
                    v.tensor_tensor(g["rt1"][:, 0:256], even, cts, op=OP.mult)
                    v.tensor_tensor(g["rt2"][:, 0:256], odd, sts, op=OP.mult)
                    v.tensor_tensor(roped[:, c0: c0 + 512: 2], g["rt1"][:, 0:256],
                                    g["rt2"][:, 0:256], op=OP.subtract)
                    v.tensor_tensor(g["rt1"][:, 0:256], odd, cts, op=OP.mult)
                    v.tensor_tensor(g["rt2"][:, 0:256], even, sts, op=OP.mult)
                    ins = v.tensor_tensor(roped[:, c0 + 1: c0 + 512: 2], g["rt1"][:, 0:256],
                                          g["rt2"][:, 0:256], op=OP.add)
                ins.then_inc(sm[tag + "rtile"], 1)
            for m in range(DCH):
                for tb in range(NTB):
                    i = m * NTB + tb
                    v.wait_ge(sm[tag + "evc"], i + 1)
                    for ts in range(4):
                        j = i * 4 + ts
                        v.wait_ge(sm[tag + "mmt"], j + 1)
                        if i >= 1:
                            v.wait_ge(sm[tag + "odma"], 16 * i)
                        ins = v.tensor_tensor(
                            outst[:, (i % 2) * 512 + ts * 128: (i % 2) * 512 + ts * 128 + 128],
                            tpp[j % 2][:], evc[:, ts * 128: ts * 128 + 128], op=OP.add)
                        ins.then_inc(sm[tag + "addt"], 1)

        @blk.gpsimd
        def _(gp):
            for m in range(DCH):
                for tb in range(NTB):
                    i = m * NTB + tb
                    gp.wait_ge(sm[tag + "addt"], (i + 1) * 4)
                    gp.dma_start(out_dram[m * 128:(m + 1) * 128, tb * TB: tb * TB + TB],
                                 outst[:, (i % 2) * 512: (i % 2) * 512 + 512]
                                 ).then_inc(sm[tag + "odma"], 16)


def emit_vtm(nc, sm, g, wap, src_ap, bias_row_ap, out_dram, tag):
    pp = [g["p_a1"], g["p_a2"]]
    p_bc = g["p_b1"]
    vst, bvvb = g["outst"], g["ln_abc"]
    with nc.Block() as blk:
        emit_wload(nc, sm, g, blk, wap, 4, 1024, 0, tag)

        @blk.sync
        def _(s):
            s.dma_start(g["bvv"][0:1, :], bias_row_ap).then_inc(sm[tag + "brow"], 16)

        @blk.tensor
        def _(te):
            te.wait_ge(sm[tag + "brow"], 16)
            te.matmul(p_bc[:, 0:512], g["ones1r"][:], g["bvv"][0:1, 0:512],
                      start=True, stop=True)
            te.matmul(g["p_b2"][:, 0:512], g["ones1r"][:], g["bvv"][0:1, 512:1024],
                      start=True, stop=True).then_inc(sm[tag + "bcmm"], 1)
            te.wait_ge(sm[tag + "wc"], 4)
            for t in range(16):
                for hf in range(2):
                    i = t * 2 + hf
                    if i >= 2:
                        te.wait_ge(sm[tag + "ev"], i - 1)
                    for kc in range(4):
                        ins = te.matmul(pp[i % 2][:, 0:512],
                                        src_ap(kc, t * 128, 128),
                                        g["wbf"][:, kc * 1024 + hf * 512: kc * 1024 + hf * 512 + 512],
                                        start=(kc == 0), stop=(kc == 3))
                    ins.then_inc(sm[tag + "mm"], 1)

        @blk.scalar
        def _(sc):
            sc.wait_ge(sm[tag + "bcmm"], 1)
            sc.copy(bvvb[:, 0:512], p_bc[:, 0:512])
            sc.copy(g["ln_bbc"][:, 0:512], g["p_b2"][:, 0:512]).then_inc(sm[tag + "bcev"], 1)

        @blk.vector
        def _(v):
            v.wait_ge(sm[tag + "bcev"], 1)
            for t in range(16):
                for hf in range(2):
                    i = t * 2 + hf
                    v.wait_ge(sm[tag + "mm"], i + 1)
                    if i >= 1:
                        v.wait_ge(sm[tag + "odma"], 16 * i)
                    bsrc = bvvb if hf == 0 else g["ln_bbc"]
                    v.tensor_tensor(vst[:, (i % 2) * 512:(i % 2) * 512 + 512],
                                    pp[i % 2][:, 0:512], bsrc[:, 0:512], op=OP.add
                                    ).then_inc(sm[tag + "ev"], 1)

        @blk.gpsimd
        def _(gp):
            for t in range(16):
                for hf in range(2):
                    i = t * 2 + hf
                    gp.wait_ge(sm[tag + "ev"], i + 1)
                    gp.dma_start(out_dram[t * 128:(t + 1) * 128, hf * 512: hf * 512 + 512],
                                 vst[:, (i % 2) * 512:(i % 2) * 512 + 512]
                                 ).then_inc(sm[tag + "odma"], 16)


L1_INS = [("x_fm", [D, T + 1], F32), ("x_last", [D, 1], F32),
           ("wdq", [D, DQ], F32), ("wdkv", [D, DKV], F32), ("wuq", [DQ, D], F32),
           ("wqr", [DQ, D], F32), ("wkr", [D, D], F32), ("wuk", [DKV, D], F32),
           ("wuv", [DKV, D], F32), ("bias_pack", [128, 40], F32),
           ("buv_row", [1, D], F32), ("ctabT", [T, D // 2], F32),
           ("stabT", [T, D // 2], F32), ("bqr_row", [1, D], F32),
           ("bkr_row", [1, D], F32), ("ident_in", [128, 128], BF16)]


def emit_l1(nc, SM, io):
    x = io["x_fm"]; x_last = io["x_last"]; wdq = io["wdq"]; wdkv = io["wdkv"]
    wuq = io["wuq"]; wqr = io["wqr"]; wkr = io["wkr"]; wuk = io["wuk"]
    wuv = io["wuv"]; bias_in = io["bias_pack"]; buv_row = io["buv_row"]
    ctabT = io["ctabT"]; stabT = io["stabT"]; bqr_row = io["bqr_row"]
    bkr_row = io["bkr_row"]; ident_in = io["ident_in"]
    q_out = io["q_out"]; k_out = io["k_out"]; vtm_out = io["vtm_out"]
    xn_out = io["xn_out"]

    ctx = ExitStack()
    sm = SM
    g = {}
    g["xn"] = sb(nc, ctx, "xn", [128, DCH * (T + 1)], BF16)
    g["ctq"] = sb(nc, ctx, "ctq", [128, 4 * T], BF16)
    g["ctkv"] = sb(nc, ctx, "ctkv", [128, 4 * T], BF16)
    g["ln_xbuf"] = [sb(nc, ctx, f"xb{i}", [128, DCH * 512], F32) for i in range(2)]
    pass_ln_sq = None
    g["ln_rows"] = sb(nc, ctx, "lnrows", [1, 512 * 6], F32)
    g["ln_abc"] = sb(nc, ctx, "lnabc", [128, 512], F32)
    g["ln_bbc"] = sb(nc, ctx, "lnbbc", [128, 512], F32)
    pass_ln_t = None
    g["ones128"] = sb(nc, ctx, "ones128", [128, 1], F32)
    g["ones1r"] = sb(nc, ctx, "ones1r", [1, 128], F32)
    g["wstage"] = [sb(nc, ctx, f"wst{i}", [128, 1024], F32) for i in range(2)]
    g["wbf"] = sb(nc, ctx, "wbf", [128, 12 * 1024], BF16)
    g["ctb"] = [sb(nc, ctx, f"ctb{i}", [128, 512], F32) for i in range(2)]
    g["stb"] = [sb(nc, ctx, f"stb{i}", [128, 512], F32) for i in range(2)]
    g["evtm"] = sb(nc, ctx, "evtm", [128, 1024], F32)
    g["roped"] = sb(nc, ctx, "roped", [128, 16 * 1024], BF16)
    g["bbias"] = sb(nc, ctx, "bbias", [128, 1024], F32)
    g["identb"] = sb(nc, ctx, "identb", [128, 128], BF16)
    g["evac"] = sb(nc, ctx, "evac", [128, 512], F32)
    g["rt1"] = sb(nc, ctx, "rt1", [128, 512], F32)
    g["ln_sq"] = g["rt1"]
    g["rt2"] = sb(nc, ctx, "rt2", [128, 512], F32)
    g["ln_t"] = g["rt2"]
    g["outst"] = sb(nc, ctx, "outst", [128, 1024], BF16)
    g["biases"] = sb(nc, ctx, "biases", [128, 40], F32)
    g["bvv"] = sb(nc, ctx, "bvv", [1, D], F32)
    g["cc"] = sb(nc, ctx, "cc", [128, 8], F32)
    g["nw"] = sb(nc, ctx, "nw", [128, 4], F32)
    g["p_a1"] = ps(nc, ctx, "p_a1", [128, 512])
    g["p_a2"] = ps(nc, ctx, "p_a2", [128, 512])
    g["p_b1"] = ps(nc, ctx, "p_b1", [128, 512])
    g["p_b2"] = ps(nc, ctx, "p_b2", [128, 512])
    g["tpp"] = [ps(nc, ctx, f"tpp{i}", [128, 128], BF16) for i in range(2)]

    BOFF = {"bdq": 0, "bdkv": 4, "buq": 8, "bqr": 16, "bkr": 24, "buk": 32}

    def bias_col(name):
        return lambda m: g["biases"][:, BOFF[name] + m: BOFF[name] + m + 1]

    def xn_ap(kc, c0, w):
        return g["xn"][:, kc * (T + 1) + 1 + c0: kc * (T + 1) + 1 + c0 + w]

    def ctkv_ap(kc, c0, w):
        return g["ctkv"][:, kc * T + c0: kc * T + c0 + w]

    def ctq_ap(kc, c0, w):
        return g["ctq"][:, kc * T + c0: kc * T + c0 + w]

    with nc.Block() as blk:
        @blk.sync
        def _(s):
            s.dma_start(g["biases"][:], bias_in).then_inc(sm["cld"], 16)
            s.dma_start(g["bvv"][:], buv_row).then_inc(sm["cld"], 16)
            s.dma_start(g["identb"][:], ident_in).then_inc(sm["cld"], 16)

        @blk.vector
        def _(v):
            v.memset(g["ones128"][:], 1.0)
            v.memset(g["ones1r"][:], 1.0)
            v.memset(g["cc"][:, 0:1], LN_EPS)
            v.memset(g["cc"][:, 1:2], 1.0)
            v.memset(g["cc"][:, 2:3], RMS_EPS)
            v.wait_ge(sm["cld"], 48)

    def ln_src(ch, c0, w):
        if w == 1 and c0 == T:
            return x_last[ch * 128:(ch + 1) * 128, 0:1]
        return x[ch * 128:(ch + 1) * 128, c0:c0 + w]
    emit_ln(nc, sm, g, ln_src, T + 1, g["xn"], T + 1, "ln")

    emit_proj_plain(nc, sm, g, wdkv, DCH, 4, xn_ap,
                    lambda m, c0, w: g["ctkv"][:, m * T + c0: m * T + c0 + w],
                    bias_col("bdkv"), "pkv")
    emit_proj_plain(nc, sm, g, wdq, DCH, 4, xn_ap,
                    lambda m, c0, w: g["ctq"][:, m * T + c0: m * T + c0 + w],
                    bias_col("bdq"), "pq")
    emit_rope_proj(nc, sm, g, wkr, DCH, wuk, 4, xn_ap, ctkv_ap,
                   bkr_row, bias_col("buk"), ctabT, stabT, k_out, "K")
    emit_vtm(nc, sm, g, wuv, ctkv_ap, buv_row, vtm_out, "V")
    emit_rope_proj(nc, sm, g, wqr, 4, wuq, 4, ctq_ap, ctq_ap,
                   bqr_row, bias_col("buq"), ctabT, stabT, q_out, "Q")

    with nc.Block() as blk:
        @blk.gpsimd
        def _(gp):
            for ch in range(DCH):
                gp.dma_start(xn_out[ch * 128:(ch + 1) * 128, :],
                             g["xn"][:, ch * (T + 1): (ch + 1) * (T + 1)]
                             ).then_inc(sm["xnod"], 16)
            gp.wait_ge(sm["xnod"], 16 * DCH)
            gp.wait_ge(sm["Kodma"], 16 * 32)
            gp.wait_ge(sm["Qodma"], 16 * 32)
            gp.wait_ge(sm["Vodma"], 16 * 32)

    ctx.close()


def build_l1():
    nc = bass.Bass("TRN2", target_bir_lowering=False)
    io = {n: nc.dram_tensor(n, s, d, kind="ExternalInput").ap() for n, s, d in L1_INS}
    io["q_out"] = nc.dram_tensor("q_out", [D, T], BF16, kind="ExternalOutput").ap()
    io["k_out"] = nc.dram_tensor("k_out", [D, T], BF16, kind="ExternalOutput").ap()
    io["vtm_out"] = nc.dram_tensor("vtm_out", [T, D], BF16, kind="ExternalOutput").ap()
    io["xn_out"] = nc.dram_tensor("xn_out", [D, T + 1], BF16, kind="ExternalOutput").ap()
    ctx = ExitStack()
    emit_l1(nc, Sems(nc, ctx), io)
    ctx.close()
    return nc



def emit_flash(nc, sm, g, q_src_ap, k_loads, va_src_ap, vb_src_ap, out_dram, tag):
    NKC = 64
    qt = g["fl_q"]
    kt = g["fl_k"]
    vt = g["fl_v"]
    P = g["fl_p"]
    s2 = g["fl_s"]
    ava, avb = g["fl_ava"], g["fl_avb"]
    bc = g["p_b1"]
    bcs = g["fl_bcs"]
    rows = g["fl_rows"]
    ast = g["outst"]

    nkl = len(k_loads(0))
    def NDMA(p):
        return (1 + nkl + 2 * NKC) * 16 * (p + 1)

    with nc.Block() as blk:
        @blk.vector
        def _(v):
            for i in range(2):
                v.memset(vt[i][:, 64::130], 1.0)
                ins = v.memset(vt[i][:, 129::130], 1.0)
            ins.then_inc(sm[tag + "vones"], 1)

        @blk.sync
        def _(s):
            for p in range(8):
                if p == 0:
                    s.wait_ge(sm[tag + "vones"], 1)
                if p >= 2:
                    s.wait_ge(sm[tag + "odma"], 16 * 4 * (p - 1))
                for cols, kap in k_loads(p):
                    s.dma_start(kt[p % 2][:, cols], kap).then_inc(sm[tag + "kv"], 16)
                s.dma_start(qt[p % 2][:], q_src_ap(p)).then_inc(sm[tag + "kv"], 16)
                for kc in range(NKC):
                    s.dma_start(vt[p % 2][:, kc * 130: kc * 130 + 64],
                                va_src_ap(p, kc)).then_inc(sm[tag + "kv"], 16)
                    s.dma_start(vt[p % 2][:, kc * 130 + 65: kc * 130 + 129],
                                vb_src_ap(p, kc)).then_inc(sm[tag + "kv"], 16)

        @blk.tensor
        def _(te):
            for p in range(8):
                for qb in range(4):
                    r = p * 4 + qb
                    if qb == 0:
                        te.wait_ge(sm[tag + "kv"], NDMA(p))
                    for kc in range(NKC):
                        gk = r * NKC + kc
                        if gk >= 2:
                            te.wait_ge(sm[tag + "exp"], gk - 1)
                        te.matmul(s2[kc % 2][:, 0:512],
                                  kt[p % 2][0:64, kc * 128:(kc + 1) * 128],
                                  qt[p % 2][0:64, qb * 512:(qb + 1) * 512],
                                  start=True, stop=True)
                        te.matmul(s2[kc % 2][:, 512:1024],
                                  kt[p % 2][64:128, kc * 128:(kc + 1) * 128],
                                  qt[p % 2][64:128, qb * 512:(qb + 1) * 512],
                                  start=True, stop=True,
                                  tile_position=(64, 0)).then_inc(sm[tag + "smm"], 1)
                        if kc >= 1:
                            j = kc - 1
                            te.wait_ge(sm[tag + "exp"], r * NKC + kc)
                            if j == 0:
                                te.wait_ge(sm[tag + "natt"], 2 * r)
                            te.matmul(ava[:], vt[p % 2][:, j * 130: j * 130 + 65],
                                      P[j % 2][:, 0:512],
                                      start=(j == 0), stop=False)
                            te.matmul(avb[:], vt[p % 2][:, j * 130 + 65: j * 130 + 130],
                                      P[j % 2][:, 512:1024],
                                      start=(j == 0), stop=False).then_inc(sm[tag + "avmm"], 1)
                    j = NKC - 1
                    te.wait_ge(sm[tag + "exp"], r * NKC + NKC)
                    te.matmul(ava[:], vt[p % 2][:, j * 130: j * 130 + 65],
                              P[j % 2][:, 0:512], start=False, stop=True)
                    te.matmul(avb[:], vt[p % 2][:, j * 130 + 65: j * 130 + 130],
                              P[j % 2][:, 512:1024], start=False,
                              stop=True).then_inc(sm[tag + "avmm"], 1)
                    te.wait_ge(sm[tag + "rcp"], r + 1)
                    if r > 0:
                        te.wait_ge(sm[tag + "bce"], r)
                    te.matmul(bc[0:64, 0:512], g["ones1r"][0:1, 0:64],
                              rows[0:1, 0:512], start=True, stop=True)
                    te.matmul(bc[64:128, 0:512], g["ones1r"][0:1, 0:64],
                              rows[0:1, 512:1024], start=True, stop=True,
                              tile_position=(0, 64)).then_inc(sm[tag + "bcm"], 1)

        @blk.scalar
        def _(sc):
            for p in range(8):
                for qb in range(4):
                    r = p * 4 + qb
                    for kc in range(NKC):
                        gk = r * NKC + kc
                        sc.wait_ge(sm[tag + "smm"], gk + 1)
                        if gk >= 2:
                            sc.wait_ge(sm[tag + "avmm"], gk - 1)
                        sc.activation(P[kc % 2][:], s2[kc % 2][:], AF.Exp,
                                      scale=SCALE0).then_inc(sm[tag + "exp"], 1)
                    sc.wait_ge(sm[tag + "bcm"], r + 1)
                    if r > 0:
                        sc.wait_ge(sm[tag + "natt"], 2 * r)
                    sc.copy(bcs[:, 0:512], bc[:, 0:512]).then_inc(sm[tag + "bce"], 1)

        @blk.vector
        def _(v):
            for p in range(8):
                for qb in range(4):
                    r = p * 4 + qb
                    v.wait_ge(sm[tag + "avmm"], (r + 1) * NKC)
                    v.reciprocal(rows[0:1, 0:512], ava[64:65, 0:512])
                    v.reciprocal(rows[0:1, 512:1024],
                                 avb[64:65, 0:512]).then_inc(sm[tag + "rcp"], 1)
                    v.wait_ge(sm[tag + "bce"], r + 1)
                    if r >= 1:
                        v.wait_ge(sm[tag + "odma"], 16 * r)
                    v.tensor_tensor(ast[0:64, (r % 2) * 512:(r % 2) * 512 + 512],
                                    ava[0:64, 0:512], bcs[0:64, 0:512],
                                    op=OP.mult).then_inc(sm[tag + "natt"], 1)
                    v.tensor_tensor(ast[64:128, (r % 2) * 512:(r % 2) * 512 + 512],
                                    avb[0:64, 0:512], bcs[64:128, 0:512],
                                    op=OP.mult).then_inc(sm[tag + "natt"], 1)

        @blk.gpsimd
        def _(gp):
            for p in range(8):
                for qb in range(4):
                    r = p * 4 + qb
                    gp.wait_ge(sm[tag + "natt"], 2 * (r + 1))
                    gp.dma_start(out_dram[p * 128:(p + 1) * 128,
                                          qb * 512:(qb + 1) * 512],
                                 ast[:, (r % 2) * 512:(r % 2) * 512 + 512]
                                 ).then_inc(sm[tag + "odma"], 16)
            gp.wait_ge(sm[tag + "odma"], 16 * 32)


def emit_l2(nc, SM, io, q_ap, k_loads, va_src_ap, vb_src_ap):
    attn_out = io["attn_out"]
    ctx = ExitStack()
    sm = SM
    g = {}
    g["fl_q"] = [sb(nc, ctx, f"flq{i}", [128, T], BF16) for i in range(2)]
    g["fl_k"] = [sb(nc, ctx, f"flk{i}", [128, S], BF16) for i in range(2)]
    g["fl_v"] = [sb(nc, ctx, f"flv{i}", [128, 64 * 130], BF16) for i in range(2)]
    g["fl_p"] = [sb(nc, ctx, f"flp{i}", [128, 1024], BF16) for i in range(2)]
    g["fl_bcs"] = sb(nc, ctx, "flbcs", [128, 512], F32)
    g["fl_rows"] = sb(nc, ctx, "flrows", [1, 1024], F32)
    g["outst"] = sb(nc, ctx, "outst", [128, 1024], BF16)
    g["ones1r"] = sb(nc, ctx, "ones1r", [1, 128], F32)
    g["fl_s"] = [ps(nc, ctx, f"fls{i}", [128, 1024]) for i in range(2)]
    g["fl_ava"] = ps(nc, ctx, "flava", [65, 512])
    g["fl_avb"] = ps(nc, ctx, "flavb", [65, 512])
    g["p_b1"] = ps(nc, ctx, "p_b1", [128, 512])

    with nc.Block() as blk:
        @blk.vector
        def _(v):
            v.memset(g["ones1r"][:], 1.0)

    emit_flash(nc, sm, g, q_ap, k_loads, va_src_ap, vb_src_ap, attn_out, "F")
    ctx.close()


def build_l2():
    nc = bass.Bass("TRN2", target_bir_lowering=False)
    q_in = nc.dram_tensor("q_in", [D, T], BF16, kind="ExternalInput").ap()
    kg = nc.dram_tensor("kg", [D, S], BF16, kind="ExternalInput").ap()
    vg = nc.dram_tensor("vg", [S, D], BF16, kind="ExternalInput").ap()
    io = {"attn_out": nc.dram_tensor("attn_out", [D, T], BF16,
                                     kind="ExternalOutput").ap()}
    ctx = ExitStack()
    emit_l2(nc, Sems(nc, ctx), io,
            lambda p: q_in[p * 128:(p + 1) * 128, :],
            lambda p: [(slice(0, S), kg[p * 128:(p + 1) * 128, :])],
            lambda p, kc: vg[kc * 128:(kc + 1) * 128, p * 128: p * 128 + 64],
            lambda p, kc: vg[kc * 128:(kc + 1) * 128, p * 128 + 64: p * 128 + 128])
    ctx.close()
    return nc



L3_INS = [("wo", [D, D], F32), ("wrT", [D, D], F32), ("wkT", [D, D], F32),
           ("wvT", [D, D], F32), ("cpack", [128, 24], F32)]


def emit_l3(nc, SM, io):
    attn_in = io["attn_in"]; xn_in = io["xn_in"]; x_in = io["x_in"]
    wo = io["wo"]; wrT = io["wrT"]; wkT = io["wkT"]; wvT = io["wvT"]
    cpack = io["cpack"]
    x1_out = io["x1_out"]; xn1_out = io["xn1_out"]

    ctx = ExitStack()
    sm = SM
    g = {}
    g["ones128"] = sb(nc, ctx, "ones128", [128, 1], F32)
    g["ones1r"] = sb(nc, ctx, "ones1r", [1, 128], F32)
    g["cc"] = sb(nc, ctx, "cc", [128, 8], F32)
    g["cp"] = sb(nc, ctx, "cp", [128, 24], F32)
    g["om"] = sb(nc, ctx, "om", [128, 16], F32)

    pctx = ExitStack()
    xn_sb = pctx.enter_context(nc.sbuf_tensor(getattr(nc, "_name_pre", "") + "xn_sb", [128, DCH * (T + 1)], BF16))
    wstage = [pctx.enter_context(nc.sbuf_tensor(getattr(nc, "_name_pre", "") + f"wst{i}", [128, 1024], F32)) for i in range(2)]
    wobf = pctx.enter_context(nc.sbuf_tensor(getattr(nc, "_name_pre", "") + "wobf", [128, DCH * 1024], BF16))
    wrbf = pctx.enter_context(nc.sbuf_tensor(getattr(nc, "_name_pre", "") + "wrbf", [128, DCH * 1024], BF16))
    wkbf = pctx.enter_context(nc.sbuf_tensor(getattr(nc, "_name_pre", "") + "wkbf", [128, DCH * 1024], BF16))
    wvbf = pctx.enter_context(nc.sbuf_tensor(getattr(nc, "_name_pre", "") + "wvbf", [128, DCH * 1024], BF16))
    attn_sb = pctx.enter_context(nc.sbuf_tensor(getattr(nc, "_name_pre", "") + "attn_sb", [128, DCH * 512], BF16))
    x_sb = [pctx.enter_context(nc.sbuf_tensor(getattr(nc, "_name_pre", "") + "x_sb0", [128, DCH * 512], F32))] * 2
    xr_sb = pctx.enter_context(nc.sbuf_tensor(getattr(nc, "_name_pre", "") + "xr_sb", [128, DCH * 512], BF16))
    xk_sb = pctx.enter_context(nc.sbuf_tensor(getattr(nc, "_name_pre", "") + "xk_sb", [128, DCH * 512], BF16))
    r_sb = pctx.enter_context(nc.sbuf_tensor(getattr(nc, "_name_pre", "") + "r_sb", [128, DCH * 512], F32))
    kact = pctx.enter_context(nc.sbuf_tensor(getattr(nc, "_name_pre", "") + "kact", [128, DCH * 512], BF16))
    s_sb = pctx.enter_context(nc.sbuf_tensor(getattr(nc, "_name_pre", "") + "s_sb", [128, DCH * 512], F32))
    krl = pctx.enter_context(nc.sbuf_tensor(getattr(nc, "_name_pre", "") + "krl", [128, 2 * 512], F32))
    ffn_t = pctx.enter_context(nc.sbuf_tensor(getattr(nc, "_name_pre", "") + "ffn_t", [128, 2 * 512], F32))
    g_t = pctx.enter_context(nc.sbuf_tensor(getattr(nc, "_name_pre", "") + "g_t", [128, 2 * 512], F32))
    t2_t = pctx.enter_context(nc.sbuf_tensor(getattr(nc, "_name_pre", "") + "t2_t", [128, 2 * 512], F32))
    x1st = pctx.enter_context(nc.sbuf_tensor(getattr(nc, "_name_pre", "") + "x1st", [128, 2 * 512], F32))
    mixs = pctx.enter_context(nc.sbuf_tensor(getattr(nc, "_name_pre", "") + "mixs", [128, 512], F32))
    sp = [pctx.enter_context(nc.psum_tensor(getattr(nc, "_name_pre", "") + f"sp{i}", [128, 512], F32)) for i in range(2)]
    rp = [pctx.enter_context(nc.psum_tensor(getattr(nc, "_name_pre", "") + f"rp{i}", [128, 512], F32)) for i in range(2)]
    kp = [pctx.enter_context(nc.psum_tensor(getattr(nc, "_name_pre", "") + f"kp{i}", [128, 512], F32)) for i in range(2)]
    vp = [pctx.enter_context(nc.psum_tensor(getattr(nc, "_name_pre", "") + f"vp{i}", [128, 512], F32)) for i in range(2)]

    def wcol(wt, kc, m):
        return wt[:, kc * 1024 + m * 128: kc * 1024 + m * 128 + 128]

    with nc.Block() as blk:
        @blk.sync
        def _(s):
            s.dma_start(g["cp"][:], cpack).then_inc(sm["cld"], 16)
            for ch in range(DCH):
                s.dma_start(xn_sb[:, ch * (T + 1):(ch + 1) * (T + 1)],
                            xn_in[ch * 128:(ch + 1) * 128, :]).then_inc(sm["cld"], 16)
            for wi, (wap, dst) in enumerate([(wo, wobf), (wrT, wrbf), (wkT, wkbf), (wvT, wvbf)]):
                for kc in range(DCH):
                    n = wi * DCH + kc
                    if n >= 2:
                        s.wait_ge(sm["wc"], n - 1)
                    s.dma_start(wstage[n % 2][:, :],
                                wap[kc * 128:(kc + 1) * 128, :]).then_inc(sm["wld"], 16)

        @blk.gpsimd
        def _(gp):
            for wi, dst in enumerate([wobf, wrbf, wkbf, wvbf]):
                for kc in range(DCH):
                    n = wi * DCH + kc
                    gp.wait_ge(sm["wld"], 16 * (n + 1))
                    gp.tensor_copy(dst[:, kc * 1024:(kc + 1) * 1024],
                                   wstage[n % 2][:, :]).then_inc(sm["wc"], 1)
            gp.wait_ge(sm["cld"], 16 * (1 + DCH))
            gp.tensor_scalar(g["om"][:, 0:8], g["cp"][:, 8:16], -1.0, 1.0,
                             op0=OP.mult, op1=OP.add)
            gp.tensor_scalar(g["om"][:, 8:16], g["cp"][:, 16:24], -1.0, 1.0,
                             op0=OP.mult, op1=OP.add).then_inc(sm["omd"], 1)

        @blk.vector
        def _(v):
            v.memset(g["ones128"][:], 1.0)
            v.memset(g["ones1r"][:], 1.0)
            v.memset(g["cc"][:, 0:1], LN_EPS)
            v.wait_ge(sm["wc"], 4 * DCH)
            v.wait_ge(sm["omd"], 1)

    with nc.Block() as blk:
        @blk.sync
        def _(s):
            for tb in range(NTB):
                if tb >= 1:
                    s.wait_ge(sm["x1v"], tb * DCH)
                for ch in range(DCH):
                    s.dma_start(x_sb[tb % 2][:, ch * 512:(ch + 1) * 512],
                                x_in[ch * 128:(ch + 1) * 128, tb * TB: tb * TB + TB]
                                ).then_inc(sm["xld"], 16)
                if tb >= 1:
                    s.wait_ge(sm["smm"], tb * DCH)
                for ch in range(DCH):
                    s.dma_start(attn_sb[:, ch * 512:(ch + 1) * 512],
                                attn_in[ch * 128:(ch + 1) * 128, tb * TB: tb * TB + TB]
                                ).then_inc(sm["ald"], 16)

        @blk.vector
        def _(v):
            for tb in range(NTB):
                if tb >= 1:
                    v.wait_ge(sm["rmm"], tb * DCH)
                    v.wait_ge(sm["kmm"], tb * DCH)
                for ch in range(DCH):
                    xc = xn_sb[:, ch * (T + 1) + 1 + tb * TB: ch * (T + 1) + 1 + tb * TB + TB]
                    xp = xn_sb[:, ch * (T + 1) + tb * TB: ch * (T + 1) + tb * TB + TB]
                    v.tensor_scalar(mixs[:], xp, g["om"][:, ch:ch + 1], None, op0=OP.mult)
                    v.scalar_tensor_tensor(xr_sb[:, ch * 512:(ch + 1) * 512], xc,
                                           g["cp"][:, 8 + ch: 8 + ch + 1], mixs[:],
                                           op0=OP.mult, op1=OP.add)
                    v.tensor_scalar(mixs[:], xp, g["om"][:, 8 + ch: 8 + ch + 1], None, op0=OP.mult)
                    ins = v.scalar_tensor_tensor(xk_sb[:, ch * 512:(ch + 1) * 512], xc,
                                                 g["cp"][:, 16 + ch: 16 + ch + 1], mixs[:],
                                                 op0=OP.mult, op1=OP.add)
                ins.then_inc(sm["mix"], 1)
                for m in range(DCH):
                    i = tb * DCH + m
                    v.wait_ge(sm["vmm"], i + 1)
                    v.tensor_tensor(ffn_t[:, (m % 2) * 512:(m % 2) * 512 + 512],
                                    vp[m % 2][:], r_sb[:, m * 512:(m + 1) * 512],
                                    op=OP.mult).then_inc(sm["vff"], 1)
                    v.wait_ge(sm["gsil"], i + 1)
                    v.tensor_tensor(t2_t[:, (m % 2) * 512:(m % 2) * 512 + 512],
                                    s_sb[:, m * 512:(m + 1) * 512],
                                    g_t[:, (m % 2) * 512:(m % 2) * 512 + 512], op=OP.mult)
                    v.wait_ge(sm["xld"], 16 * (tb * DCH + DCH))
                    if i >= 1:
                        v.wait_ge(sm["x1w"], 16 * i)
                    v.tensor_tensor(x1st[:, (i % 2) * 512:(i % 2) * 512 + 512],
                                    x_sb[tb % 2][:, m * 512:(m + 1) * 512],
                                    t2_t[:, (m % 2) * 512:(m % 2) * 512 + 512],
                                    op=OP.add).then_inc(sm["x1v"], 1)

        @blk.tensor
        def _(te):
            for tb in range(NTB):
                te.wait_ge(sm["ald"], 16 * DCH * (tb + 1))
                for m in range(DCH):
                    i = tb * DCH + m
                    if i >= 2:
                        te.wait_ge(sm["sev"], i - 1)
                    for kc in range(DCH):
                        ins = te.matmul(sp[m % 2][:], wcol(wobf, kc, m),
                                        attn_sb[:, kc * 512:(kc + 1) * 512],
                                        start=(kc == 0), stop=(kc == DCH - 1))
                    ins.then_inc(sm["smm"], 1)
                te.wait_ge(sm["mix"], tb + 1)
                for m in range(DCH):
                    i = tb * DCH + m
                    if i >= 2:
                        te.wait_ge(sm["rev"], i - 1)
                    for kc in range(DCH):
                        ins = te.matmul(rp[m % 2][:], wcol(wrbf, kc, m),
                                        xr_sb[:, kc * 512:(kc + 1) * 512],
                                        start=(kc == 0), stop=(kc == DCH - 1))
                    ins.then_inc(sm["rmm"], 1)
                for m in range(DCH):
                    i = tb * DCH + m
                    if i >= 2:
                        te.wait_ge(sm["ksq"], i - 1)
                    for kc in range(DCH):
                        ins = te.matmul(kp[m % 2][:], wcol(wkbf, kc, m),
                                        xk_sb[:, kc * 512:(kc + 1) * 512],
                                        start=(kc == 0), stop=(kc == DCH - 1))
                    ins.then_inc(sm["kmm"], 1)
                te.wait_ge(sm["ksq"], (tb + 1) * DCH)
                for m in range(DCH):
                    i = tb * DCH + m
                    if i >= 2:
                        te.wait_ge(sm["vff"], i - 1)
                    for kc in range(DCH):
                        ins = te.matmul(vp[m % 2][:], wcol(wvbf, kc, m),
                                        kact[:, kc * 512:(kc + 1) * 512],
                                        start=(kc == 0), stop=(kc == DCH - 1))
                    ins.then_inc(sm["vmm"], 1)

        @blk.scalar
        def _(sc):
            for tb in range(NTB):
                for m in range(DCH):
                    i = tb * DCH + m
                    sc.wait_ge(sm["smm"], i + 1)
                    if tb >= 1:
                        sc.wait_ge(sm["x1v"], (tb - 1) * DCH + m + 1)
                    sc.activation(s_sb[:, m * 512:(m + 1) * 512], sp[m % 2][:],
                                  AF.Identity, bias=g["cp"][:, m:m + 1]
                                  ).then_inc(sm["sev"], 1)
                for m in range(DCH):
                    i = tb * DCH + m
                    sc.wait_ge(sm["rmm"], i + 1)
                    if tb >= 1:
                        sc.wait_ge(sm["vff"], (tb - 1) * DCH + m + 1)
                    sc.activation(r_sb[:, m * 512:(m + 1) * 512], rp[m % 2][:],
                                  AF.Sigmoid).then_inc(sm["rev"], 1)
                for m in range(DCH):
                    i = tb * DCH + m
                    sc.wait_ge(sm["kmm"], i + 1)
                    if tb >= 1:
                        sc.wait_ge(sm["vmm"], tb * DCH)
                    sc.activation(krl[:, (m % 2) * 512:(m % 2) * 512 + 512],
                                  kp[m % 2][:], AF.Relu)
                    sc.activation(kact[:, m * 512:(m + 1) * 512],
                                  krl[:, (m % 2) * 512:(m % 2) * 512 + 512],
                                  AF.Square).then_inc(sm["ksq"], 1)
                for m in range(DCH):
                    i = tb * DCH + m
                    sc.wait_ge(sm["vff"], i + 1)
                    sc.activation(g_t[:, (m % 2) * 512:(m % 2) * 512 + 512],
                                  ffn_t[:, (m % 2) * 512:(m % 2) * 512 + 512],
                                  AF.Silu).then_inc(sm["gsil"], 1)

        @blk.gpsimd
        def _(gp):
            for tb in range(NTB):
                for m in range(DCH):
                    i = tb * DCH + m
                    gp.wait_ge(sm["x1v"], i + 1)
                    gp.dma_start(x1_out[m * 128:(m + 1) * 128, tb * TB: tb * TB + TB],
                                 x1st[:, (i % 2) * 512:(i % 2) * 512 + 512]
                                 ).then_inc(sm["x1w"], 16)
            gp.wait_ge(sm["x1w"], 16 * 4 * DCH)

    pctx.close()

    lctx = ExitStack()
    g["xn1"] = lctx.enter_context(nc.sbuf_tensor(getattr(nc, "_name_pre", "") + "xn1t", [128, DCH * T], BF16))
    g["ln_xbuf"] = [lctx.enter_context(nc.sbuf_tensor(getattr(nc, "_name_pre", "") + f"lxb{i}", [128, DCH * 512], F32)) for i in range(2)]
    g["ln_sq"] = lctx.enter_context(nc.sbuf_tensor(getattr(nc, "_name_pre", "") + "lsq", [128, 512], F32))
    g["ln_rows"] = lctx.enter_context(nc.sbuf_tensor(getattr(nc, "_name_pre", "") + "lrows", [1, 512 * 6], F32))
    g["ln_abc"] = lctx.enter_context(nc.sbuf_tensor(getattr(nc, "_name_pre", "") + "labc", [128, 512], F32))
    g["ln_bbc"] = lctx.enter_context(nc.sbuf_tensor(getattr(nc, "_name_pre", "") + "lbbc", [128, 512], F32))
    g["ln_t"] = lctx.enter_context(nc.sbuf_tensor(getattr(nc, "_name_pre", "") + "lnt", [128, 512], F32))
    g["nw"] = lctx.enter_context(nc.sbuf_tensor(getattr(nc, "_name_pre", "") + "nw", [128, 4], F32))
    g["p_a1"] = lctx.enter_context(nc.psum_tensor(getattr(nc, "_name_pre", "") + "p_a1", [128, 512], F32))
    g["p_a2"] = lctx.enter_context(nc.psum_tensor(getattr(nc, "_name_pre", "") + "p_a2", [128, 512], F32))
    g["p_b1"] = lctx.enter_context(nc.psum_tensor(getattr(nc, "_name_pre", "") + "p_b1", [128, 512], F32))
    g["p_b2"] = lctx.enter_context(nc.psum_tensor(getattr(nc, "_name_pre", "") + "p_b2", [128, 512], F32))

    emit_ln(nc, sm, g, lambda ch, c0, w: x1_out[ch * 128:(ch + 1) * 128, c0:c0 + w],
            T, g["xn1"], T, "l1n", wait_first=("x1w", 16 * 4 * DCH))

    with nc.Block() as blk:
        @blk.gpsimd
        def _(gp):
            for ch in range(DCH):
                gp.dma_start(xn1_out[ch * 128:(ch + 1) * 128, :],
                             g["xn1"][:, ch * T:(ch + 1) * T]).then_inc(sm["xn1od"], 16)
            gp.wait_ge(sm["xn1od"], 16 * DCH)

    lctx.close()
    ctx.close()


def build_l3():
    nc = bass.Bass("TRN2", target_bir_lowering=False)
    io = {n: nc.dram_tensor(n, s, d, kind="ExternalInput").ap() for n, s, d in L3_INS}
    io["attn_in"] = nc.dram_tensor("attn_in", [D, T], BF16, kind="ExternalInput").ap()
    io["xn_in"] = nc.dram_tensor("xn_in", [D, T + 1], BF16, kind="ExternalInput").ap()
    io["x_in"] = nc.dram_tensor("x_in", [D, T], F32, kind="ExternalInput").ap()
    io["x1_out"] = nc.dram_tensor("x1_out", [D, T], F32, kind="ExternalOutput").ap()
    io["xn1_out"] = nc.dram_tensor("xn1_out", [D, T], BF16, kind="ExternalOutput").ap()
    ctx = ExitStack()
    emit_l3(nc, Sems(nc, ctx), io)
    ctx.close()
    return nc



NCH = 32
CH = 64


L4A_INS = [("wqgT", [D, D], F32), ("wkgT", [D, D], F32), ("wvgT", [D, D], F32),
            ("mask_in", [CH, CH], BF16)]


def emit_l4a(nc, SM, io):
    xn1_in = io["xn1_in"]; wqgT = io["wqgT"]; wkgT = io["wkgT"]
    wvgT = io["wvgT"]; mask_in = io["mask_in"]; ident_in = io["ident_in"]
    o_out = io["o_out"]; qtil_out = io["qtil_out"]; bg_out = io["bg_out"]
    gch_out = io["gch_out"]

    ctx = ExitStack()
    sm = SM
    g = {}
    xn1 = sb(nc, ctx, "xn1", [128, DCH * T], BF16)
    vtm = sb(nc, ctx, "vtm", [128, 16 * 1024], BF16)
    wstage = [sb(nc, ctx, f"wst{i}", [128, 1024], F32) for i in range(2)]
    wv_bf = sb(nc, ctx, "wv_bf", [128, DCH * 1024], BF16)
    wq_bf = sb(nc, ctx, "wq_bf", [128, DCH * 1024], BF16)
    wk_bf = sb(nc, ctx, "wk_bf", [128, DCH * 1024], BF16)
    mask64 = sb(nc, ctx, "mask64", [CH, CH], BF16)
    identb = sb(nc, ctx, "identb", [128, 128], BF16)
    ones2k = sb(nc, ctx, "ones2k", [128, T], F32)
    cc = sb(nc, ctx, "cc", [128, 8], F32)
    q_f = sb(nc, ctx, "q_f", [128, T], F32)
    f_f = sb(nc, ctx, "f_f", [128, T], F32)
    sp_f = sb(nc, ctx, "sp_f", [128, T], F32)
    A_f = sb(nc, ctx, "A_f", [128, T], F32)
    k_f = sb(nc, ctx, "k_f", [128, T], F32)
    qtil = sb(nc, ctx, "qtil", [128, T], BF16)
    o_f = sb(nc, ctx, "o_f", [128, T], F32)
    e1 = sb(nc, ctx, "e1", [128, CH], F32)
    e2 = sb(nc, ctx, "e2", [128, CH], F32)
    tloc = sb(nc, ctx, "tloc", [128, CH], F32)
    ktf = sb(nc, ctx, "ktf", [128, CH], F32)
    ktil = sb(nc, ctx, "ktil", [128, CH], BF16)
    khat = sb(nc, ctx, "khat", [128, CH], BF16)
    khat_tm = sb(nc, ctx, "khat_tm", [128, 128], BF16)
    S_bf = sb(nc, ctx, "S_bf", [128, 128], BF16)
    Pm = sb(nc, ctx, "Pm", [64, CH], BF16)
    vch = [sb(nc, ctx, f"vch{i}", [64, 128], BF16) for i in range(2)]
    B_f = sb(nc, ctx, "B_f", [128, 128], F32)
    Bt_f = sb(nc, ctx, "Bt_f", [128, 128], F32)
    gch = sb(nc, ctx, "gch", [128, NCH], F32)
    gtot = sb(nc, ctx, "gtot", [128, CH], F32)
    bgst = sb(nc, ctx, "bgst", [128, 132], F32)
    zcol = sb(nc, ctx, "zcol", [128, 1], F32)
    pj = [ps(nc, ctx, f"pj{i}", [128, 512]) for i in range(2)]
    tpp = ps(nc, ctx, "tpp", [64, 128], BF16)
    kvp = ps(nc, ctx, "kvp", [128, 128])
    ppp = ps(nc, ctx, "ppp", [64, CH])
    opp = [ps(nc, ctx, f"opp{i}", [128, CH]) for i in range(2)]

    def wcol(wt, kc, m):
        return wt[:, kc * 1024 + m * 128: kc * 1024 + m * 128 + 128]

    with nc.Block() as blk:
        @blk.sync
        def _(s):
            s.dma_start(mask64[:], mask_in).then_inc(sm["cld"], 16)
            s.dma_start(identb[:], ident_in).then_inc(sm["cld"], 16)
            for ch in range(DCH):
                s.dma_start(xn1[:, ch * T:(ch + 1) * T],
                            xn1_in[ch * 128:(ch + 1) * 128, :]).then_inc(sm["cld"], 16)
            for wi, (wap, dst) in enumerate([(wvgT, wv_bf), (wqgT, wq_bf), (wkgT, wk_bf)]):
                for kc in range(DCH):
                    n = wi * DCH + kc
                    if n >= 2:
                        s.wait_ge(sm["wc"], n - 1)
                    s.dma_start(wstage[n % 2][:], wap[kc * 128:(kc + 1) * 128, :]
                                ).then_inc(sm["wld"], 16)

        @blk.gpsimd
        def _(gp):
            for wi, dst in enumerate([wv_bf, wq_bf, wk_bf]):
                for kc in range(DCH):
                    n = wi * DCH + kc
                    gp.wait_ge(sm["wld"], 16 * (n + 1))
                    gp.tensor_copy(dst[:, kc * 1024:(kc + 1) * 1024],
                                   wstage[n % 2][:]).then_inc(sm["wc"], 1)

        @blk.vector
        def _(v):
            v.memset(ones2k[:], 1.0)
            v.memset(cc[:, 0:1], 1.0)
            v.memset(bgst[:, 128:132], 0.0)
            v.memset(zcol[:], 0.0).then_inc(sm["msd"], 1)

        @blk.tensor
        def _(te):
            te.wait_ge(sm["wc"], DCH)
            te.wait_ge(sm["cld"], 16 * (2 + DCH))
            for t in range(16):
                for hf in range(2):
                    i = t * 2 + hf
                    if i >= 2:
                        te.wait_ge(sm["vev"], i - 1)
                    for kc in range(DCH):
                        ins = te.matmul(pj[i % 2][:],
                                        xn1[:, kc * T + t * 128: kc * T + t * 128 + 128],
                                        wv_bf[:, kc * 1024 + hf * 512: kc * 1024 + hf * 512 + 512],
                                        start=(kc == 0), stop=(kc == DCH - 1))
                    ins.then_inc(sm["vmm"], 1)

        @blk.vector
        def _(v):
            for t in range(16):
                for hf in range(2):
                    i = t * 2 + hf
                    v.wait_ge(sm["vmm"], i + 1)
                    v.tensor_copy(vtm[:, t * 1024 + hf * 512: t * 1024 + hf * 512 + 512],
                                  pj[i % 2][:]).then_inc(sm["vev"], 1)

    with nc.Block() as blk:
        @blk.tensor
        def _(te):
            for h in range(DCH):
                for pi, wt in enumerate([wq_bf, wk_bf]):
                    for tb in range(NTB):
                        i = h * 8 + pi * 4 + tb
                        if i >= 2:
                            te.wait_ge(sm["pjev"], i - 1)
                        for kc in range(DCH):
                            ins = te.matmul(pj[i % 2][:],
                                            wcol(wt, kc, h),
                                            xn1[:, kc * T + tb * TB: kc * T + tb * TB + TB],
                                            start=(kc == 0), stop=(kc == DCH - 1))
                        ins.then_inc(sm["pjmm"], 1)
                for c in range(NCH):
                    hc = h * NCH + c
                    par = c % 2
                    te.wait_ge(sm["khb"], hc + 1)
                    if hc >= 1:
                        te.wait_ge(sm["khtm"], hc)
                        te.wait_ge(sm["pmask"], hc)
                    te.transpose(tpp[:], khat[:], identb[:]).then_inc(sm["tpm"], 1)
                    te.wait_ge(sm["khtm"], hc + 1)
                    te.wait_ge(sm["vchd"], 16 * (hc + 1))
                    te.wait_ge(sm["bchain"], hc)
                    te.matmul(kvp[:], khat_tm[0:64, :], vch[par][:],
                              start=True, stop=True)
                    te.matmul(ppp[:], ktil[:], qtil[:, c * CH:(c + 1) * CH],
                              start=True, stop=True).then_inc(sm["ppm"], 1)
                    te.wait_ge(sm["pmask"], hc + 1)
                    te.wait_ge(sm["sbf"], hc + 1)
                    if hc >= 2:
                        te.wait_ge(sm["oev"], hc - 1)
                    te.matmul(opp[par][:], vch[par][:], Pm[:],
                              start=True, stop=False)
                    te.matmul(opp[par][:], S_bf[:], qtil[:, c * CH:(c + 1) * CH],
                              start=False, stop=True).then_inc(sm["opm"], 1)

        @blk.scalar
        def _(sc):
            for h in range(DCH):
                if h >= 1:
                    sc.wait_ge(sm["khb"], h * NCH)
                for pi in range(2):
                    for tb in range(NTB):
                        i = h * 8 + pi * 4 + tb
                        sc.wait_ge(sm["pjmm"], i + 1)
                        dst = (q_f if pi == 0 else f_f)
                        fn = AF.Silu if pi == 0 else AF.Sigmoid
                        sc.activation(dst[:, tb * TB: tb * TB + TB], pj[i % 2][:],
                                      fn).then_inc(sm["pjev"], 1)
                if h >= 1:
                    sc.wait_ge(sm["scand"], h)
                for tb in range(NTB):
                    sc.wait_ge(sm["pjev"], h * 8 + 4 + tb + 1)
                    sc.activation(sp_f[:, tb * TB: tb * TB + TB],
                                  f_f[:, tb * TB: tb * TB + TB], AF.Exp, scale=-1.0)
                    sc.activation(sp_f[:, tb * TB: tb * TB + TB],
                                  sp_f[:, tb * TB: tb * TB + TB], AF.Ln,
                                  bias=cc[:, 0:1]).then_inc(sm["spd"], 1)
                for c in range(NCH):
                    hc = h * NCH + c
                    sc.wait_ge(sm["tlocd"], hc + 1)
                    if hc >= 1:
                        sc.wait_ge(sm["khtm"], hc)
                    sc.activation(e1[:], tloc[:], AF.Exp)
                    sc.activation(e2[:], tloc[:], AF.Exp, scale=-1.0).then_inc(sm["e12"], 1)
                sc.wait_ge(sm["scand"], h + 1)
                if h >= 1:
                    sc.wait_ge(sm["bgd"], h)
                sc.activation(gtot[:], A_f[:, T - CH: T], AF.Exp).then_inc(sm["gtd"], 1)

        @blk.vector
        def _(v):
            for h in range(DCH):
                if h >= 1:
                    v.wait_ge(sm["od"], 16 * 4 * h)
                v.wait_ge(sm["pjev"], h * 8 + 8)
                for tb in range(NTB):
                    v.tensor_scalar(k_f[:, tb * TB: tb * TB + TB],
                                    f_f[:, tb * TB: tb * TB + TB], -1.0, 1.0,
                                    op0=OP.mult, op1=OP.add)
                v.wait_ge(sm["spd"], h * 4 + 4)
                v.tensor_tensor_scan(A_f[:], ones2k[:], sp_f[:], initial=0.0,
                                     op0=OP.mult, op1=OP.subtract).then_inc(sm["scand"], 1)
                v.memset(B_f[:], 0.0).then_inc(sm["bzero"], 1)
                for c in range(NCH):
                    hc = h * NCH + c
                    aoff = zcol[:, 0:1] if c == 0 else A_f[:, c * CH - 1: c * CH]
                    v.tensor_scalar(tloc[:], A_f[:, c * CH:(c + 1) * CH], aoff, None,
                                    op0=OP.subtract).then_inc(sm["tlocd"], 1)
                    v.wait_ge(sm["e12"], hc + 1)
                    v.tensor_tensor(qtil[:, c * CH:(c + 1) * CH],
                                    q_f[:, c * CH:(c + 1) * CH], e1[:], op=OP.mult)
                    v.tensor_tensor(ktf[:], k_f[:, c * CH:(c + 1) * CH], e2[:], op=OP.mult)
                    v.tensor_copy(ktil[:], ktf[:])
                    v.tensor_scalar(khat[:], ktf[:], e1[:, CH - 1: CH], None,
                                    op0=OP.mult).then_inc(sm["khb"], 1)
                    v.tensor_copy(gch[:, c: c + 1], e1[:, CH - 1: CH])
                    v.wait_ge(sm["tpm"], hc + 1)
                    v.tensor_copy(khat_tm[0:64, :], tpp[0:64, :]).then_inc(sm["khtm"], 1)
                    v.wait_ge(sm["ppm"], hc + 1)
                    v.tensor_tensor(Pm[:], ppp[0:64, :], mask64[:], op=OP.mult
                                    ).then_inc(sm["pmask"], 1)
                    if c == 0:
                        v.wait_ge(sm["bzero"], h + 1)
                    v.tensor_copy(S_bf[:], B_f[:]).then_inc(sm["sbf"], 1)
                    v.tensor_scalar(Bt_f[:], B_f[:], e1[:, CH - 1: CH], None, op0=OP.mult)
                    v.tensor_tensor(B_f[:], Bt_f[:], kvp[:], op=OP.add
                                    ).then_inc(sm["bchain"], 1)
                    v.wait_ge(sm["opm"], hc + 1)
                    v.tensor_copy(o_f[:, c * CH:(c + 1) * CH], opp[c % 2][:]
                                  ).then_inc(sm["oev"], 1)
                v.wait_ge(sm["gtd"], h + 1)
                v.tensor_copy(bgst[:, 0:128], B_f[:])
                v.tensor_copy(bgst[:, 128:129], gtot[:, CH - 1: CH]
                              ).then_inc(sm["bgd"], 1)

        @blk.gpsimd
        def _(gp):
            for h in range(DCH):
                for c in range(NCH):
                    hc = h * NCH + c
                    if hc == 0:
                        gp.wait_ge(sm["vev"], 32)
                    if hc >= 2:
                        gp.wait_ge(sm["opm"], hc - 1)
                    gp.dma_start(vch[c % 2][:],
                                 vtm[(c % 2) * 64:(c % 2) * 64 + 64,
                                     (c // 2) * 1024 + h * 128:(c // 2) * 1024 + (h + 1) * 128]
                                 ).then_inc(sm["vchd"], 16)
                gp.wait_ge(sm["oev"], (h + 1) * NCH)
                gp.dma_start(o_out[h * 128:(h + 1) * 128, :], o_f[:]).then_inc(sm["od"], 16)
                gp.dma_start(qtil_out[h * 128:(h + 1) * 128, :], qtil[:]).then_inc(sm["od"], 16)
                gp.dma_start(gch_out[h * 128:(h + 1) * 128, :], gch[:]).then_inc(sm["od"], 16)
                gp.wait_ge(sm["bgd"], h + 1)
                gp.dma_start(bg_out[h * 128:(h + 1) * 128, :], bgst[:]).then_inc(sm["od"], 16)
            gp.wait_ge(sm["od"], 16 * 4 * DCH)

    ctx.close()


def build_l4a():
    nc = bass.Bass("TRN2", target_bir_lowering=False)
    io = {n: nc.dram_tensor(n, s, d, kind="ExternalInput").ap() for n, s, d in L4A_INS}
    io["ident_in"] = nc.dram_tensor("ident_in", [128, 128], BF16, kind="ExternalInput").ap()
    io["xn1_in"] = nc.dram_tensor("xn1_in", [D, T], BF16, kind="ExternalInput").ap()
    io["o_out"] = nc.dram_tensor("o_out", [D, T], F32, kind="ExternalOutput").ap()
    io["qtil_out"] = nc.dram_tensor("qtil_out", [D, T], BF16, kind="ExternalOutput").ap()
    io["bg_out"] = nc.dram_tensor("bg_out", [D, 132], F32, kind="ExternalOutput").ap()
    io["gch_out"] = nc.dram_tensor("gch_out", [D, NCH], F32, kind="ExternalOutput").ap()
    ctx = ExitStack()
    emit_l4a(nc, Sems(nc, ctx), io)
    ctx.close()
    return nc


L4B_INS = [("mpack", [128, 16], F32), ("wogT", [D, D], F32)]


def emit_l4b(nc, SM, io):
    o_in = io["o_in"]; qtil_in = io["qtil_in"]; gch_in = io["gch_in"]
    ccg = io["ccg"]; mpack = io["mpack"]; wogT = io["wogT"]
    s1_out = io["s1_out"]

    ctx = ExitStack()
    sm = SM
    o_sb = sb(nc, ctx, "o_sb", [128, DCH * T], F32)
    qtil = sb(nc, ctx, "qtil", [128, T], BF16)
    gch = sb(nc, ctx, "gch", [128, NCH], F32)
    Bi = sb(nc, ctx, "Bi", [128, 132], F32)
    mk = sb(nc, ctx, "mk", [128, 16], F32)
    T_f = sb(nc, ctx, "T_f", [128, 128], F32)
    T_bf = sb(nc, ctx, "T_bf", [128, 128], BF16)
    Bm = sb(nc, ctx, "Bm", [128, 132], F32)
    wstage = [sb(nc, ctx, f"wst{i}", [128, 1024], F32) for i in range(2)]
    wo_bf = sb(nc, ctx, "wo_bf", [128, DCH * 1024], BF16)
    normed = sb(nc, ctx, "normed", [128, DCH * 512], BF16)
    sq = sb(nc, ctx, "sq", [128, 512], F32)
    rows = sb(nc, ctx, "rows", [1, 512 * 3], F32)
    rbc = sb(nc, ctx, "rbc", [128, 512], F32)
    outst = sb(nc, ctx, "outst", [128, 1024], BF16)
    ones128 = sb(nc, ctx, "ones128", [128, 1], F32)
    ones1r = sb(nc, ctx, "ones1r", [1, 128], F32)
    cc = sb(nc, ctx, "cc", [128, 8], F32)
    corr = [ps(nc, ctx, f"corr{i}", [128, CH]) for i in range(2)]
    p_s = ps(nc, ctx, "p_s", [1, 512])
    p_bc = ps(nc, ctx, "p_bc", [128, 512])
    pj = [ps(nc, ctx, f"pj{i}", [128, 512]) for i in range(2)]

    with nc.Block() as blk:
        @blk.sync
        def _(s):
            s.dma_start(mk[:], mpack).then_inc(sm["cld"], 16)
            for ch in range(DCH):
                s.dma_start(o_sb[:, ch * T:(ch + 1) * T],
                            o_in[ch * 128:(ch + 1) * 128, :]).then_inc(sm["cld"], 16)
            for kc in range(DCH):
                if kc >= 2:
                    s.wait_ge(sm["wc"], kc - 1)
                s.dma_start(wstage[kc % 2][:], wogT[kc * 128:(kc + 1) * 128, :]
                            ).then_inc(sm["wld"], 16)

        @blk.gpsimd
        def _(gp):
            for kc in range(DCH):
                gp.wait_ge(sm["wld"], 16 * (kc + 1))
                gp.tensor_copy(wo_bf[:, kc * 1024:(kc + 1) * 1024],
                               wstage[kc % 2][:]).then_inc(sm["wc"], 1)

        @blk.vector
        def _(v):
            v.memset(ones128[:], 1.0)
            v.memset(ones1r[:], 1.0)
            v.memset(cc[:, 0:1], RMS_EPS).then_inc(sm["msd"], 1)

    with nc.Block() as blk:
        @blk.sync
        def _(s):
            for h in range(DCH):
                if h >= 1:
                    s.wait_ge(sm["hdone"], h)
                s.dma_start(qtil[:], qtil_in[h * 128:(h + 1) * 128, :]).then_inc(sm["hld"], 16)
                s.dma_start(gch[:], gch_in[h * 128:(h + 1) * 128, :]).then_inc(sm["hld"], 16)
                for i in range(RANKS):
                    s.dma_start(Bi[:],
                                ccg[i * D + h * 128: i * D + (h + 1) * 128, :]
                                ).then_inc(sm["bld"], 16)
                    s.wait_ge(sm["bused"], h * RANKS + i + 1)

        @blk.vector
        def _(v):
            v.wait_ge(sm["cld"], 16 * (1 + DCH))
            v.wait_ge(sm["msd"], 1)
            for h in range(DCH):
                v.wait_ge(sm["hld"], 32 * (h + 1))
                v.memset(T_f[:], 0.0)
                for i in range(RANKS):
                    v.wait_ge(sm["bld"], 16 * (h * RANKS + i + 1))
                    v.tensor_scalar(Bm[:, 0:1], Bi[:, 128:129], mk[:, i:i + 1],
                                    mk[:, 4 + i:4 + i + 1], op0=OP.mult,
                                    op1=OP.add).then_inc(sm["gmd"], 1)
                    v.tensor_scalar(Bm[:, 4:132], Bi[:, 0:128], mk[:, i:i + 1],
                                    None, op0=OP.mult)
                    v.wait_ge(sm["gmd"], h * RANKS + i + 1)
                    v.tensor_scalar(T_f[:], T_f[:], Bm[:, 0:1], None, op0=OP.mult)
                    v.tensor_tensor(T_f[:], T_f[:], Bm[:, 4:132], op=OP.add
                                    ).then_inc(sm["bused"], 1)
                for c in range(NCH):
                    hc = h * NCH + c
                    v.tensor_copy(T_bf[:], T_f[:]).then_inc(sm["tbf"], 1)
                    v.wait_ge(sm["cmm"], hc + 1)
                    v.tensor_tensor(o_sb[:, h * T + c * CH: h * T + (c + 1) * CH],
                                    o_sb[:, h * T + c * CH: h * T + (c + 1) * CH],
                                    corr[c % 2][:], op=OP.add).then_inc(sm["oadd"], 1)
                    ins = v.tensor_scalar(T_f[:], T_f[:], gch[:, c:c + 1], None,
                                          op0=OP.mult)
                ins.then_inc(sm["hdone"], 1)

        @blk.tensor
        def _(te):
            for h in range(DCH):
                for c in range(NCH):
                    hc = h * NCH + c
                    te.wait_ge(sm["tbf"], hc + 1)
                    if hc >= 2:
                        te.wait_ge(sm["oadd"], hc - 1)
                    te.matmul(corr[c % 2][:], T_bf[:], qtil[:, c * CH:(c + 1) * CH],
                              start=True, stop=True).then_inc(sm["cmm"], 1)

    with nc.Block() as blk:
        @blk.scalar
        def _(sc):
            for tb in range(NTB):
                for ch in range(DCH):
                    if tb > 0 or ch > 0:
                        sc.wait_ge(sm["rsmm"], tb * DCH + ch)
                    sc.activation(sq[:], o_sb[:, ch * T + tb * TB: ch * T + tb * TB + TB],
                                  AF.Square).then_inc(sm["rsq"], 1)
                sc.wait_ge(sm["rv"], tb + 1)
                sc.activation(rows[0:1, 512:1024], rows[0:1, 0:512], AF.Ln, bias=cc[0:1, 0:1])
                sc.activation(rows[0:1, 1024:1536], rows[0:1, 512:1024], AF.Exp,
                              scale=-0.5).then_inc(sm["rstd"], 1)
                sc.wait_ge(sm["rbcm"], tb + 1)
                if tb >= 1:
                    sc.wait_ge(sm["nrm"], tb * DCH)
                sc.copy(rbc[:], p_bc[:]).then_inc(sm["rbce"], 1)

        @blk.tensor
        def _(te):
            for tb in range(NTB):
                for ch in range(DCH):
                    te.wait_ge(sm["rsq"], tb * DCH + ch + 1)
                    if ch == 0 and tb > 0:
                        te.wait_ge(sm["rv"], tb)
                    te.matmul(p_s[0:1, :], ones128[:], sq[:],
                              start=(ch == 0), stop=(ch == DCH - 1)).then_inc(sm["rsmm"], 1)
                te.wait_ge(sm["rstd"], tb + 1)
                if tb > 0:
                    te.wait_ge(sm["rbce"], tb)
                te.matmul(p_bc[:], ones1r[:], rows[0:1, 1024:1536],
                          start=True, stop=True).then_inc(sm["rbcm"], 1)
                for m in range(DCH):
                    i = tb * DCH + m
                    te.wait_ge(sm["nrm"], (tb + 1) * DCH)
                    if i >= 2:
                        te.wait_ge(sm["sev"], i - 1)
                    for kc in range(DCH):
                        ins = te.matmul(pj[m % 2][:], wcol_g(wo_bf, kc, m),
                                        normed[:, kc * 512:(kc + 1) * 512],
                                        start=(kc == 0), stop=(kc == DCH - 1))
                    ins.then_inc(sm["smm"], 1)

        @blk.vector
        def _(v):
            for tb in range(NTB):
                v.wait_ge(sm["rsmm"], (tb + 1) * DCH)
                if tb >= 1:
                    v.wait_ge(sm["rstd"], tb)
                v.tensor_scalar(rows[0:1, 0:512], p_s[0:1, :], 1.0 / D, None,
                                op0=OP.mult).then_inc(sm["rv"], 1)
                v.wait_ge(sm["rbce"], tb + 1)
                if tb >= 1:
                    v.wait_ge(sm["smm"], tb * DCH)
                for ch in range(DCH):
                    v.scalar_tensor_tensor(normed[:, ch * 512:(ch + 1) * 512],
                                           o_sb[:, ch * T + tb * TB: ch * T + tb * TB + TB],
                                           mk[:, 8 + ch: 8 + ch + 1], rbc[:],
                                           op0=OP.mult, op1=OP.mult).then_inc(sm["nrm"], 1)
                for m in range(DCH):
                    i = tb * DCH + m
                    v.wait_ge(sm["smm"], i + 1)
                    if i >= 1:
                        v.wait_ge(sm["sod"], 16 * i)
                    v.tensor_copy(outst[:, (i % 2) * 512:(i % 2) * 512 + 512],
                                  pj[m % 2][:]).then_inc(sm["sev"], 1)

        @blk.gpsimd
        def _(gp):
            for tb in range(NTB):
                for m in range(DCH):
                    i = tb * DCH + m
                    gp.wait_ge(sm["sev"], i + 1)
                    gp.dma_start(s1_out[m * 128:(m + 1) * 128, tb * TB: tb * TB + TB],
                                 outst[:, (i % 2) * 512:(i % 2) * 512 + 512]
                                 ).then_inc(sm["sod"], 16)
            gp.wait_ge(sm["sod"], 16 * 4 * DCH)

    ctx.close()


def build_l4b():
    nc = bass.Bass("TRN2", target_bir_lowering=False)
    io = {n: nc.dram_tensor(n, s, d, kind="ExternalInput").ap() for n, s, d in L4B_INS}
    io["o_in"] = nc.dram_tensor("o_in", [D, T], F32, kind="ExternalInput").ap()
    io["qtil_in"] = nc.dram_tensor("qtil_in", [D, T], BF16, kind="ExternalInput").ap()
    io["gch_in"] = nc.dram_tensor("gch_in", [D, NCH], F32, kind="ExternalInput").ap()
    io["ccg"] = nc.dram_tensor("ccg", [RANKS * D, 132], F32, kind="ExternalInput").ap()
    io["s1_out"] = nc.dram_tensor("s1_out", [D, T], BF16, kind="ExternalOutput").ap()
    ctx = ExitStack()
    emit_l4b(nc, Sems(nc, ctx), io)
    ctx.close()
    return nc


def wcol_g(wt, kc, m):
    return wt[:, kc * 1024 + m * 128: kc * 1024 + m * 128 + 128]



NE = 8
F_ = 2048
NFC = F_ // 128


L5_INS = [("rwT", [D, NE], F32), ("rb_row", [1, NE], F32),
           ("upT", [9 * D, F_], F32), ("dnT", [9 * F_, D], F32)]


def emit_l5(nc, SM, io):
    xn1_in = io["xn1_in"]; x1_in = io["x1_in"]; s1_in = io["s1_in"]
    rwT = io["rwT"]; rb_row = io["rb_row"]; upT = io["upT"]; dnT = io["dnT"]
    ident_in = io["ident_in"]; out_fm = io["out_fm"]

    ctx = ExitStack()
    sm = SM
    xn1 = sb(nc, ctx, "xn1", [128, DCH * T], BF16)
    identb = sb(nc, ctx, "identb", [128, 128], BF16)
    ones1r = sb(nc, ctx, "ones1r", [1, 128], F32)
    ones1rb = sb(nc, ctx, "ones1rb", [1, 128], BF16)
    rw_bf = sb(nc, ctx, "rw_bf", [128, DCH * NE], BF16)
    rbb = sb(nc, ctx, "rbb", [128, NE], F32)
    rst = sb(nc, ctx, "rst", [128, 64], F32)
    comb = sb(nc, ctx, "comb", [128, NE], BF16)
    comb8 = sb(nc, ctx, "comb8", [NE, 128], BF16)
    comb_rows = sb(nc, ctx, "comb_rows", [1, NE * T], BF16)
    cbc = [sb(nc, ctx, f"cbc{i}", [128, T], BF16) for i in range(2)]
    wstu = [sb(nc, ctx, f"wstu{i}", [128, DCH * 128], F32) for i in range(2)]
    wupb = [sb(nc, ctx, f"wupb{i}", [128, DCH * 128], BF16) for i in range(2)]
    wstd = [sb(nc, ctx, f"wstd{i}", [128, NFC * 128], F32) for i in range(2)]
    wdnb = [sb(nc, ctx, f"wdnb{i}", [128, NFC * 128], BF16) for i in range(2)]
    sh = sb(nc, ctx, "sh", [128, NFC * 1024], BF16)
    dacc = sb(nc, ctx, "dacc", [128, DCH * 1024], F32)
    silu_t = sb(nc, ctx, "silu_t", [128, 2 * 512], BF16)
    g1_t = sb(nc, ctx, "g1_t", [128, 512], F32)
    t2_t = sb(nc, ctx, "t2_t", [128, 512], F32)
    x1_t = sb(nc, ctx, "x1_t", [128, 512], F32)
    s1_t = sb(nc, ctx, "s1_t", [128, 512], BF16)
    outst = sb(nc, ctx, "outst", [128, 2 * 512], F32)
    tps = ps(nc, ctx, "tps", [NE, 128], BF16)
    bcp = ps(nc, ctx, "bcp", [128, 512])
    rps = bcp
    hp = [ps(nc, ctx, f"hp{i}", [128, 512]) for i in range(2)]
    dp = [ps(nc, ctx, f"dp{i}", [128, 512]) for i in range(2)]

    def HS(eng, name, ins):
        n = sm.__class__ and None
        ins.then_inc(sm[name], 1)
        return sm[name]

    with nc.Block() as blk:
        @blk.sync
        def _(s):
            s.dma_start(identb[:], ident_in).then_inc(sm["cld"], 16)
            s.dma_start(rst[0:1, 0:NE], rb_row).then_inc(sm["cld"], 16)
            for ch in range(DCH):
                s.dma_start(xn1[:, ch * T:(ch + 1) * T],
                            xn1_in[ch * 128:(ch + 1) * 128, :]).then_inc(sm["cld"], 16)
            for kc in range(DCH):
                s.dma_start(wstu[0][:, kc * NE:(kc + 1) * NE],
                            rwT[kc * 128:(kc + 1) * 128, :]).then_inc(sm["rwl"], 16)

        @blk.gpsimd
        def _(gp):
            gp.wait_ge(sm["rwl"], 16 * DCH)
            for kc in range(DCH):
                gp.tensor_copy(rw_bf[:, kc * NE:(kc + 1) * NE],
                               wstu[0][:, kc * NE:(kc + 1) * NE]).then_inc(sm["rwc"], 1)

        @blk.vector
        def _(v):
            v.memset(ones1r[:], 1.0)
            v.memset(ones1rb[:], 1.0).then_inc(sm["msd"], 1)

        @blk.tensor
        def _(te):
            te.wait_ge(sm["cld"], 16 * (2 + DCH))
            te.wait_ge(sm["msd"], 1)
            te.matmul(rps[:, 0:NE], ones1r[:], rst[0:1, 0:NE], start=True,
                      stop=True).then_inc(sm["rbm"], 1)

        @blk.scalar
        def _(sc):
            sc.wait_ge(sm["rbm"], 1)
            sc.copy(rbb[:], rps[:, 0:NE]).then_inc(sm["rbe"], 1)

    with nc.Block() as blk:
        @blk.tensor
        def _(te):
            for t in range(16):
                if t >= 1:
                    te.wait_ge(sm["lgd"], t)
                te.wait_ge(sm["rwc"], DCH)
                for kc in range(DCH):
                    te.matmul(rps[:, 0:NE], xn1[:, kc * T + t * 128: kc * T + t * 128 + 128],
                              rw_bf[:, kc * NE:(kc + 1) * NE],
                              start=(kc == 0), stop=(kc == DCH - 1)).then_inc(sm["rmm"], 1)
                te.wait_ge(sm["cmb"], t + 1)
                if t >= 1:
                    te.wait_ge(sm["c8d"], t)
                te.transpose(tps[:], comb[:, 0:NE].rearrange("p n -> p n")
                             if False else comb[:], identb[0:NE, 0:128]
                             if False else identb[:]).then_inc(sm["ctm"], 1)

        @blk.vector
        def _(v):
            C = {"mx": 48, "nm": 49, "sm": 50, "rs": 51, "m1": 52, "m2": 53,
                 "den": 54, "rd": 55}
            hs = [0]

            def col(n):
                return rst[:, C[n]:C[n] + 1]

            def step(ins):
                ins.then_inc(sm["hs"], 1)
                hs[0] += 1
                v.wait_ge(sm["hs"], hs[0])

            for t in range(16):
                v.wait_ge(sm["rmm"], (t + 1) * DCH)
                v.wait_ge(sm["rbe"], 1)
                step(v.tensor_tensor(rst[:, 0:8], rps[:, 0:NE], rbb[:], op=OP.add))
                step(v.reduce_max(col("mx"), rst[:, 0:8], axis=mybir.AxisListType.X))
                v.tensor_scalar(col("nm"), col("mx"), -1.0, None, op0=OP.mult
                                ).then_inc(sm["lgd"], 1)
                v.wait_ge(sm["exd"], t + 1)
                step(v.reduce_sum(col("sm"), rst[:, 8:16], axis=mybir.AxisListType.X))
                step(v.reciprocal(col("rs"), col("sm")))
                step(v.tensor_scalar(rst[:, 16:24], rst[:, 8:16], col("rs"), None, op0=OP.mult))
                step(v.reduce_max(col("m1"), rst[:, 16:24], axis=mybir.AxisListType.X))
                step(v.tensor_scalar(rst[:, 24:32], rst[:, 16:24], col("m1"), None,
                                     op0=OP.is_equal))
                step(v.tensor_tensor(rst[:, 32:40], rst[:, 24:32], rst[:, 16:24], op=OP.mult))
                step(v.tensor_tensor(rst[:, 40:48], rst[:, 16:24], rst[:, 32:40],
                                     op=OP.subtract))
                step(v.reduce_max(col("m2"), rst[:, 40:48], axis=mybir.AxisListType.X))
                step(v.tensor_scalar(col("den"), col("m1"), col("m2"), None, op0=OP.add))
                step(v.reciprocal(col("rd"), col("den")))
                step(v.tensor_scalar(rst[:, 24:32], rst[:, 16:24], col("m2"), None,
                                     op0=OP.is_ge))
                step(v.scalar_tensor_tensor(comb[:], rst[:, 16:24], col("rd"), rst[:, 24:32],
                                            op0=OP.mult, op1=OP.mult))
                v.nop().then_inc(sm["cmb"], 1)
                v.wait_ge(sm["ctm"], t + 1)
                if t >= 1:
                    v.wait_ge(sm["crd"], 16 * NE * t)
                v.tensor_copy(comb8[:], tps[:]).then_inc(sm["c8d"], 1)

        @blk.scalar
        def _(sc):
            for t in range(16):
                sc.wait_ge(sm["lgd"], t + 1)
                sc.activation(rst[:, 8:16], rst[:, 0:8], AF.Exp,
                              bias=rst[:, 49:50]).then_inc(sm["exd"], 1)

        @blk.gpsimd
        def _(gp):
            for t in range(16):
                gp.wait_ge(sm["c8d"], t + 1)
                for e in range(NE):
                    gp.dma_start(comb_rows[0:1, e * T + t * 128: e * T + (t + 1) * 128],
                                 comb8[e:e + 1, :]).then_inc(sm["crd"], 16)

    with nc.Block() as blk:
        @blk.sync
        def _(s):
            for p in range(2):
                for e in range(9):
                    for Fc in range(NFC):
                        n = (p * 9 + e) * NFC + Fc
                        if n >= 2:
                            s.wait_ge(sm["upc"], n - 1)
                        for kc in range(DCH):
                            s.dma_start(wstu[n % 2][:, kc * 128:(kc + 1) * 128],
                                        upT[e * D + kc * 128: e * D + (kc + 1) * 128,
                                            Fc * 128:(Fc + 1) * 128]).then_inc(sm["upl"], 16)
                    for Dc in range(DCH):
                        n = (p * 9 + e) * DCH + Dc
                        if n >= 2:
                            s.wait_ge(sm["dnc"], n - 1)
                        for Fc in range(NFC):
                            s.dma_start(wstd[n % 2][:, Fc * 128:(Fc + 1) * 128],
                                        dnT[e * F_ + Fc * 128: e * F_ + (Fc + 1) * 128,
                                            Dc * 128:(Dc + 1) * 128]).then_inc(sm["dnl"], 16)

        @blk.gpsimd
        def _(gp):
            for p in range(2):
                for e in range(9):
                    for Fc in range(NFC):
                        n = (p * 9 + e) * NFC + Fc
                        gp.wait_ge(sm["upl"], 16 * DCH * (n + 1))
                        gp.wait_ge(sm["upmm"], max(0, (n - 1) * 2))
                        gp.tensor_copy(wupb[n % 2][:], wstu[n % 2][:]).then_inc(sm["upc"], 1)
                    for Dc in range(DCH):
                        n = (p * 9 + e) * DCH + Dc
                        gp.wait_ge(sm["dnl"], 16 * NFC * (n + 1))
                        gp.wait_ge(sm["dnmm"], max(0, (n - 1) * 2))
                        gp.tensor_copy(wdnb[n % 2][:], wstd[n % 2][:]).then_inc(sm["dnc"], 1)
                for Dc in range(DCH):
                    for half in range(2):
                        i = (p * DCH + Dc) * 2 + half
                        gp.wait_ge(sm["fin"], i + 1)
                        c0 = (p * 2 + half) * 512
                        gp.dma_start(out_fm[Dc * 128:(Dc + 1) * 128, c0:c0 + 512],
                                     outst[:, (i % 2) * 512:(i % 2) * 512 + 512]
                                     ).then_inc(sm["fod"], 16)
            gp.wait_ge(sm["fod"], 16 * 2 * DCH * 2)

        @blk.tensor
        def _(te):
            for p in range(2):
                for e in range(9):
                    pe = p * 9 + e
                    pb = p * NE + e
                    if e < NE:
                        te.wait_ge(sm["crd"], 16 * 16 * NE)
                        for half in range(2):
                            te.matmul(bcp[:, 0:512], ones1rb[:],
                                      comb_rows[0:1, e * T + (p * 2 + half) * 512:
                                                e * T + (p * 2 + half) * 512 + 512],
                                      start=True, stop=True).then_inc(sm["cbm"], 1)
                            te.wait_ge(sm["cbe"], 2 * pb + half + 1)
                    for Fc in range(NFC):
                        n = pe * NFC + Fc
                        te.wait_ge(sm["upc"], n + 1)
                        for kc in range(DCH):
                            for half in range(2):
                                i2 = Fc * 2 + half
                                if kc == 0 and n * 2 + half >= 2:
                                    te.wait_ge(sm["shd"], n * 2 + half - 1)
                                ins = te.matmul(hp[i2 % 2][:],
                                          wupb[n % 2][:, kc * 128:(kc + 1) * 128],
                                          xn1[:, kc * T + (p * 2 + half) * 512:
                                              kc * T + (p * 2 + half) * 512 + 512],
                                          start=(kc == 0), stop=(kc == DCH - 1))
                                if kc == DCH - 1:
                                    ins.then_inc(sm["upmm"], 1)
                    for Dc in range(DCH):
                        n = pe * DCH + Dc
                        te.wait_ge(sm["dnc"], n + 1)
                        te.wait_ge(sm["shd"], pe * NFC * 2 + NFC * 2)
                        for Fc in range(NFC):
                            for half in range(2):
                                i2 = Dc * 2 + half
                                if Fc == 0 and n * 2 + half >= 2:
                                    te.wait_ge(sm["dacc"], n * 2 + half - 1)
                                ins = te.matmul(dp[i2 % 2][:],
                                          wdnb[n % 2][:, Fc * 128:(Fc + 1) * 128],
                                          sh[:, Fc * 1024 + half * 512:
                                             Fc * 1024 + half * 512 + 512],
                                          start=(Fc == 0), stop=(Fc == NFC - 1))
                                if Fc == NFC - 1:
                                    ins.then_inc(sm["dnmm"], 1)

        @blk.scalar
        def _(sc):
            for p in range(2):
                for e in range(9):
                    pe = p * 9 + e
                    pb = p * NE + e
                    if e < NE:
                        for half in range(2):
                            sc.wait_ge(sm["cbm"], 2 * pb + half + 1)
                            if pe >= 1:
                                sc.wait_ge(sm["shd"], pe * NFC * 2)
                            sc.copy(cbc[0][:, half * 512: half * 512 + 512],
                                    bcp[:, 0:512]).then_inc(sm["cbe"], 1)
                    for Fc in range(NFC):
                        n = pe * NFC + Fc
                        for half in range(2):
                            sc.wait_ge(sm["upmm"], n * 2 + half + 1)
                            sc.wait_ge(sm["shd"], max(0, n * 2 + half - 1))
                            sc.activation(silu_t[:, half * 512: half * 512 + 512],
                                          hp[(Fc * 2 + half) % 2][:], AF.Silu
                                          ).then_inc(sm["sil"], 1)
                for Dc in range(DCH):
                    for half in range(2):
                        i = (p * DCH + Dc) * 2 + half
                        if i >= 1:
                            sc.wait_ge(sm["fin"], i)
                        c0 = (p * 2 + half) * 512
                        sc.dma_start(x1_t[:], x1_in[Dc * 128:(Dc + 1) * 128, c0:c0 + 512]
                                     ).then_inc(sm["fld"], 16)
                        sc.dma_start(s1_t[:], s1_in[Dc * 128:(Dc + 1) * 128, c0:c0 + 512]
                                     ).then_inc(sm["fld"], 16)
                        sc.wait_ge(sm["dacc"], (p * 9 + 8) * DCH * 2 + Dc * 2 + half + 1)
                        sc.activation(g1_t[:],
                                      dacc[:, Dc * 1024 + half * 512: Dc * 1024 + half * 512 + 512],
                                      AF.Silu).then_inc(sm["g1d"], 1)

        @blk.vector
        def _(v):
            for p in range(2):
                for e in range(9):
                    pe = p * 9 + e
                    for Fc in range(NFC):
                        n = pe * NFC + Fc
                        for half in range(2):
                            if Fc == 0 and half == 0 and pe >= 1:
                                v.wait_ge(sm["dnmm"], pe * DCH * 2)
                            v.wait_ge(sm["sil"], n * 2 + half + 1)
                            dst = sh[:, Fc * 1024 + half * 512: Fc * 1024 + half * 512 + 512]
                            if e < NE:
                                v.wait_ge(sm["cbe"], 2 * (p * NE + e) + half + 1)
                                v.tensor_tensor(dst, silu_t[:, half * 512: half * 512 + 512],
                                                cbc[0][:, half * 512: half * 512 + 512],
                                                op=OP.mult).then_inc(sm["shd"], 1)
                            else:
                                v.tensor_copy(dst, silu_t[:, half * 512: half * 512 + 512]
                                              ).then_inc(sm["shd"], 1)
                    for Dc in range(DCH):
                        n = pe * DCH + Dc
                        for half in range(2):
                            v.wait_ge(sm["dnmm"], n * 2 + half + 1)
                            dst = dacc[:, Dc * 1024 + half * 512: Dc * 1024 + half * 512 + 512]
                            if e == 0:
                                v.tensor_copy(dst, dp[(Dc * 2 + half) % 2][:]
                                              ).then_inc(sm["dacc"], 1)
                            else:
                                v.tensor_tensor(dst, dst, dp[(Dc * 2 + half) % 2][:],
                                                op=OP.add).then_inc(sm["dacc"], 1)
                for Dc in range(DCH):
                    for half in range(2):
                        i = (p * DCH + Dc) * 2 + half
                        v.wait_ge(sm["g1d"], i + 1)
                        v.wait_ge(sm["fld"], 16 * 2 * (i + 1))
                        v.tensor_tensor(t2_t[:], s1_t[:], g1_t[:], op=OP.mult)
                        if i >= 1:
                            v.wait_ge(sm["fod"], 16 * i)
                        v.tensor_tensor(outst[:, (i % 2) * 512:(i % 2) * 512 + 512],
                                        x1_t[:], t2_t[:], op=OP.add).then_inc(sm["fin"], 1)

    ctx.close()


def build_l5():
    nc = bass.Bass("TRN2", target_bir_lowering=False)
    io = {n: nc.dram_tensor(n, s, d, kind="ExternalInput").ap() for n, s, d in L5_INS}
    io["ident_in"] = nc.dram_tensor("ident_in", [128, 128], BF16, kind="ExternalInput").ap()
    io["xn1_in"] = nc.dram_tensor("xn1_in", [D, T], BF16, kind="ExternalInput").ap()
    io["x1_in"] = nc.dram_tensor("x1_in", [D, T], F32, kind="ExternalInput").ap()
    io["s1_in"] = nc.dram_tensor("s1_in", [D, T], BF16, kind="ExternalInput").ap()
    io["out_fm"] = nc.dram_tensor("out_fm", [D, T], F32, kind="ExternalOutput").ap()
    ctx = ExitStack()
    emit_l5(nc, Sems(nc, ctx), io)
    ctx.close()
    return nc


sm_hs_count = [0]


RG = [[0, 1, 2, 3], [4, 5, 6, 7]]


def build_fused():
    nc = bass.Bass("TRN2", target_bir_lowering=False)
    io = {}
    for n, s, d in L1_INS + L3_INS + L4A_INS + L4B_INS + L5_INS:
        io[n] = nc.dram_tensor(n, s, d, kind="ExternalInput").ap()
    out_fm = nc.dram_tensor("out_fm", [D, T], F32, kind="ExternalOutput").ap()

    q_d = nc.dram_tensor("q_d", [D, T], BF16, kind="Internal").ap()
    k_d = nc.dram_tensor("k_d", [D, T], BF16, kind="Internal")
    v_d = nc.dram_tensor("v_d", [T, D], BF16, kind="Internal")
    kg_d = nc.dram_tensor("kg_d", [RANKS * D, T], BF16, kind="Internal")
    vg_d = nc.dram_tensor("vg_d", [RANKS * T, D], BF16, kind="Internal")
    xn_d = nc.dram_tensor("xn_d", [D, T + 1], BF16, kind="Internal").ap()
    attn_d = nc.dram_tensor("attn_d", [D, T], BF16, kind="Internal").ap()
    x1_d = nc.dram_tensor("x1_d", [D, T], F32, kind="Internal").ap()
    xn1_d = nc.dram_tensor("xn1_d", [D, T], BF16, kind="Internal").ap()
    o_d = nc.dram_tensor("o_d", [D, T], F32, kind="Internal").ap()
    qtil_d = nc.dram_tensor("qtil_d", [D, T], BF16, kind="Internal").ap()
    gch_d = nc.dram_tensor("gch_d", [D, NCH], F32, kind="Internal").ap()
    bg_d = nc.dram_tensor("bg_d", [D, 132], F32, kind="Internal")
    ccg_d = nc.dram_tensor("ccg_d", [RANKS * D, 132], F32, kind="Internal")
    s1_d = nc.dram_tensor("s1_d", [D, T], BF16, kind="Internal").ap()

    ctx = ExitStack()
    sm = Sems(nc, ctx)
    pool = [ctx.enter_context(nc.semaphore(f"pool{i}")) for i in range(48)]

    io1 = dict(io)
    io1.update({"q_out": q_d, "k_out": k_d.ap(), "vtm_out": v_d.ap(),
                "xn_out": xn_d})
    nc._name_pre = "a_"
    emit_l1(nc, PSems(pool, sm, "a_", ("cld", "lnxld", "pkvwld", "pqwld", "Krwld", "Kcwld", "Ktab", "Kbrow", "Vwld", "Vbrow", "Qrwld", "Qcwld", "Qtab", "Qbrow", "Kodma", "Qodma", "Vodma", "xnod")), io1)

    with nc.Block() as blk:
        @blk.gpsimd
        def _(gp):
            gp.collective_compute("AllGather", OP.bypass, replica_groups=RG,
                                  ins=[k_d.ap().opt()], outs=[kg_d.ap().opt()]
                                  ).then_inc(sm["cc1"], 1)
            gp.collective_compute("AllGather", OP.bypass, replica_groups=RG,
                                  ins=[v_d.ap().opt()], outs=[vg_d.ap().opt()]
                                  ).then_inc(sm["cc1"], 1)
            gp.wait_ge(sm["cc1"], 2)

    kg = kg_d.ap()
    vg = vg_d.ap()
    clear_pool(nc, pool)
    nc._name_pre = "b_"
    emit_l2(nc, PSems(pool, sm, "b_", ("Fkv", "Fodma")), {"attn_out": attn_d},
            lambda p: q_d[p * 128:(p + 1) * 128, :],
            lambda p: [(slice(r * T, (r + 1) * T),
                        kg[r * D + p * 128: r * D + (p + 1) * 128, :])
                       for r in range(RANKS)],
            lambda p, kc: vg[(kc // 16) * T + (kc % 16) * 128:
                             (kc // 16) * T + (kc % 16) * 128 + 128,
                             p * 128: p * 128 + 64],
            lambda p, kc: vg[(kc // 16) * T + (kc % 16) * 128:
                             (kc // 16) * T + (kc % 16) * 128 + 128,
                             p * 128 + 64: p * 128 + 128])

    io3 = dict(io)
    io3.update({"attn_in": attn_d, "xn_in": xn_d, "x_in": io["x_fm"][:, 1:],
                "x1_out": x1_d, "xn1_out": xn1_d})
    clear_pool(nc, pool)
    nc._name_pre = "c_"
    emit_l3(nc, PSems(pool, sm, "c_", ("cld", "wld", "xld", "ald", "x1w", "xn1od", "l1nxld")), io3)

    io4 = dict(io)
    io4.update({"xn1_in": xn1_d, "o_out": o_d, "qtil_out": qtil_d,
                "bg_out": bg_d.ap(), "gch_out": gch_d})
    clear_pool(nc, pool)
    nc._name_pre = "d_"
    emit_l4a(nc, PSems(pool, sm, "d_", ("cld", "wld", "vchd", "od")), io4)

    with nc.Block() as blk:
        @blk.gpsimd
        def _(gp):
            gp.collective_compute("AllGather", OP.bypass, replica_groups=RG,
                                  ins=[bg_d.ap().opt()], outs=[ccg_d.ap().opt()]
                                  ).then_inc(sm["cc2"], 1)
            gp.wait_ge(sm["cc2"], 1)

    io5 = dict(io)
    io5.update({"o_in": o_d, "qtil_in": qtil_d, "gch_in": gch_d,
                "ccg": ccg_d.ap(), "s1_out": s1_d})
    clear_pool(nc, pool)
    nc._name_pre = "e_"
    emit_l4b(nc, PSems(pool, sm, "e_", ("cld", "wld", "hld", "bld", "sod")), io5)

    io6 = dict(io)
    io6.update({"xn1_in": xn1_d, "x1_in": x1_d, "s1_in": s1_d,
                "out_fm": out_fm})
    clear_pool(nc, pool)
    nc._name_pre = "f_"
    emit_l5(nc, PSems(pool, sm, "f_", ("cld", "rwl", "upl", "dnl", "fld", "crd", "fod")), io6)

    ctx.close()
    return nc


def prep_fused_inputs(inputs):
    f32 = np.float32
    maps1 = prep_l1_inputs(inputs)
    cpack = np.zeros((128, 24), dtype=f32)
    cpack[:, 0:8] = np.asarray(inputs["L0_attn_Wo_b"], f32).reshape(8, 128).T
    cpack[:, 8:16] = np.asarray(inputs["L0_ffn_mu_r"], f32).reshape(8, 128).T
    cpack[:, 16:24] = np.asarray(inputs["L0_ffn_mu_k"], f32).reshape(8, 128).T
    mask = np.ascontiguousarray(np.tril(np.ones((CH, CH), f32)).T).astype(BF)
    nw = np.asarray(inputs["L1_attn_norm_w"], f32)
    w_up = np.asarray(inputs["L1_ffn_w_up"], f32)
    w_down = np.asarray(inputs["L1_ffn_w_down"], f32)
    sw_up = np.asarray(inputs["L1_ffn_shared_w_up"], f32)
    sw_down = np.asarray(inputs["L1_ffn_shared_w_down"], f32)
    shared = {
        "wo": _f32(inputs["L0_attn_Wo_w"]),
        "wrT": _f32(np.asarray(inputs["L0_ffn_W_r"], f32).T),
        "wkT": _f32(np.asarray(inputs["L0_ffn_W_k"], f32).T),
        "wvT": _f32(np.asarray(inputs["L0_ffn_W_v"], f32).T),
        "cpack": cpack,
        "wqgT": _f32(np.asarray(inputs["L1_attn_Wq"], f32).T),
        "wkgT": _f32(np.asarray(inputs["L1_attn_Wk"], f32).T),
        "wvgT": _f32(np.asarray(inputs["L1_attn_Wv"], f32).T),
        "mask_in": mask,
        "wogT": _f32(np.asarray(inputs["L1_attn_Wo"], f32).T),
        "rwT": _f32(np.asarray(inputs["L1_ffn_router_w"], f32).T),
        "rb_row": _f32(np.asarray(inputs["L1_ffn_router_b"], f32)[None, :]),
        "upT": np.concatenate(
            [np.ascontiguousarray(w_up[e].T) for e in range(NE)] +
            [np.ascontiguousarray(sw_up[0].T)], axis=0),
        "dnT": np.concatenate(
            [np.ascontiguousarray(w_down[e].T) for e in range(NE)] +
            [np.ascontiguousarray(sw_down[0].T)], axis=0),
    }
    maps = []
    for core in range(NCORES):
        b, r = divmod(core, RANKS)
        mpack = np.zeros((128, 16), dtype=f32)
        mpack[:, 0:4] = (np.arange(4) < r).astype(f32)[None, :]
        mpack[:, 4:8] = 1.0 - mpack[:, 0:4]
        mpack[:, 8:16] = nw.reshape(8, 128).T
        m = dict(shared)
        m.update(maps1[core])
        m["mpack"] = mpack
        maps.append(m)
    return maps


def kernel(**inputs):
    maps = prep_fused_inputs(inputs)
    res = run_bass_kernel_spmd(build_fused(), maps, core_ids=list(range(NCORES))).results
    out = np.empty((B, S, D), dtype=np.float32)
    for core in range(NCORES):
        b, r = divmod(core, RANKS)
        out[b, r * T:(r + 1) * T, :] = res[core]["out_fm"].T
    return out


def _f32(a):
    return np.ascontiguousarray(a, dtype=np.float32)


def rope_tables(pos0):
    ids = np.arange(D // 2, dtype=np.float32)
    theta = np.power(np.float32(10000.0), np.float32(-2.0) * ids,
                     dtype=np.float32) / np.float32(D // 2)
    pos = np.arange(pos0, pos0 + T, dtype=np.float32)[:, None]
    rmat = (pos * theta[None, :]).astype(np.float32)
    return _f32(np.cos(rmat)), _f32(np.sin(rmat))


def bias_pack(inputs):
    pack = np.zeros((128, 40), dtype=np.float32)
    def put(col, b, nch):
        pack[:, col:col + nch] = np.asarray(b, dtype=np.float32).reshape(nch, 128).T
    put(0, inputs["L0_attn_Wd_q_b"], 4)
    put(4, inputs["L0_attn_Wd_kv_b"], 4)
    put(8, inputs["L0_attn_Wu_q_b"], 8)
    put(32, inputs["L0_attn_Wu_k_b"], 8)
    return pack


def prep_l1_inputs(inputs):
    x = np.asarray(inputs["L0_attn_x"], dtype=np.float32)
    shared = {
        "wdq": _f32(inputs["L0_attn_Wd_q_w"]),
        "wdkv": _f32(inputs["L0_attn_Wd_kv_w"]),
        "wuq": _f32(inputs["L0_attn_Wu_q_w"]),
        "wqr": _f32(inputs["L0_attn_Wq_r_w"]),
        "wkr": _f32(inputs["L0_attn_Wk_r_w"]),
        "wuk": _f32(inputs["L0_attn_Wu_k_w"]),
        "wuv": _f32(inputs["L0_attn_Wu_v_w"]),
        "bias_pack": bias_pack(inputs),
        "buv_row": _f32(np.asarray(inputs["L0_attn_Wu_v_b"], dtype=np.float32)[None, :]),
        "bqr_row": _f32(np.asarray(inputs["L0_attn_Wq_r_b"], dtype=np.float32)[None, :]),
        "bkr_row": _f32(np.asarray(inputs["L0_attn_Wk_r_b"], dtype=np.float32)[None, :]),
        "ident_in": np.eye(128, dtype=BF),
    }
    maps = []
    for core in range(NCORES):
        b, r = divmod(core, RANKS)
        t0 = r * T
        xc = np.zeros((D, T + 1), dtype=np.float32)
        xc[:, 1:] = x[b, t0:t0 + T, :].T
        if t0 > 0:
            xc[:, 0] = x[b, t0 - 1, :]
        ctabT, stabT = rope_tables(t0)
        m = dict(shared)
        m.update({"x_fm": xc, "x_last": np.ascontiguousarray(xc[:, -1:]),
                  "ctabT": ctabT, "stabT": stabT})
        maps.append(m)
    return maps


def _run(nc, maps):
    return run_bass_kernel_spmd(nc, maps, core_ids=list(range(NCORES))).results


def kernel_unfused(**inputs):
    cores = list(range(NCORES))
    f32 = np.float32

    res1 = _run(build_l1(), prep_l1_inputs(inputs))

    ident = np.eye(128, dtype=BF)
    kgs, vgs = [], []
    for b in range(B):
        kgs.append(np.concatenate(
            [res1[b * RANKS + r]["k_out"] for r in range(RANKS)], axis=1))
        vgs.append(np.concatenate(
            [res1[b * RANKS + r]["vtm_out"] for r in range(RANKS)], axis=0))
    maps2 = []
    for core in cores:
        b = core // RANKS
        maps2.append({"q_in": res1[core]["q_out"], "kg": kgs[b], "vg": vgs[b]})
    res2 = _run(build_l2(), maps2)

    x = np.asarray(inputs["L0_attn_x"], dtype=f32)
    cpack = np.zeros((128, 24), dtype=f32)
    cpack[:, 0:8] = np.asarray(inputs["L0_attn_Wo_b"], f32).reshape(8, 128).T
    cpack[:, 8:16] = np.asarray(inputs["L0_ffn_mu_r"], f32).reshape(8, 128).T
    cpack[:, 16:24] = np.asarray(inputs["L0_ffn_mu_k"], f32).reshape(8, 128).T
    w3 = {
        "wo": _f32(inputs["L0_attn_Wo_w"]),
        "wrT": _f32(np.asarray(inputs["L0_ffn_W_r"], f32).T),
        "wkT": _f32(np.asarray(inputs["L0_ffn_W_k"], f32).T),
        "wvT": _f32(np.asarray(inputs["L0_ffn_W_v"], f32).T),
        "cpack": cpack,
    }
    maps3 = []
    for core in cores:
        b, r = divmod(core, RANKS)
        t0 = r * T
        m = dict(w3)
        m.update({
            "attn_in": res2[core]["attn_out"],
            "xn_in": res1[core]["xn_out"],
            "x_in": np.ascontiguousarray(x[b, t0:t0 + T, :].T),
        })
        maps3.append(m)
    res3 = _run(build_l3(), maps3)

    mask = np.ascontiguousarray(np.tril(np.ones((CH, CH), f32)).T).astype(BF)
    w4 = {
        "wqgT": _f32(np.asarray(inputs["L1_attn_Wq"], f32).T),
        "wkgT": _f32(np.asarray(inputs["L1_attn_Wk"], f32).T),
        "wvgT": _f32(np.asarray(inputs["L1_attn_Wv"], f32).T),
        "mask_in": mask, "ident_in": ident,
    }
    maps4 = []
    for core in cores:
        m = dict(w4)
        m.update({"xn1_in": res3[core]["xn1_out"]})
        maps4.append(m)
    res4a = _run(build_l4a(), maps4)

    nw = np.asarray(inputs["L1_attn_norm_w"], f32)
    ccgs = [np.ascontiguousarray(np.concatenate(
        [res4a[b * RANKS + r]["bg_out"] for r in range(RANKS)], axis=0))
        for b in range(B)]
    wogT = _f32(np.asarray(inputs["L1_attn_Wo"], f32).T)
    maps4b = []
    for core in cores:
        b, r = divmod(core, RANKS)
        mpack = np.zeros((128, 16), dtype=f32)
        mpack[:, 0:4] = (np.arange(4) < r).astype(f32)[None, :]
        mpack[:, 4:8] = 1.0 - mpack[:, 0:4]
        mpack[:, 8:16] = nw.reshape(8, 128).T
        ra = res4a[core]
        maps4b.append({
            "o_in": ra["o_out"], "qtil_in": ra["qtil_out"], "gch_in": ra["gch_out"],
            "ccg": ccgs[b], "mpack": mpack, "wogT": wogT,
        })
    res4b = _run(build_l4b(), maps4b)

    w_up = np.asarray(inputs["L1_ffn_w_up"], f32)
    w_down = np.asarray(inputs["L1_ffn_w_down"], f32)
    sw_up = np.asarray(inputs["L1_ffn_shared_w_up"], f32)
    sw_down = np.asarray(inputs["L1_ffn_shared_w_down"], f32)
    upT = np.concatenate([np.ascontiguousarray(w_up[e].T) for e in range(NE)] +
                         [np.ascontiguousarray(sw_up[0].T)], axis=0)
    dnT = np.concatenate([np.ascontiguousarray(w_down[e].T) for e in range(NE)] +
                         [np.ascontiguousarray(sw_down[0].T)], axis=0)
    w5 = {
        "rwT": _f32(np.asarray(inputs["L1_ffn_router_w"], f32).T),
        "rb_row": _f32(np.asarray(inputs["L1_ffn_router_b"], f32)[None, :]),
        "upT": upT, "dnT": dnT, "ident_in": ident,
    }
    maps5 = []
    for core in cores:
        m = dict(w5)
        m.update({
            "xn1_in": res3[core]["xn1_out"],
            "x1_in": res3[core]["x1_out"],
            "s1_in": res4b[core]["s1_out"],
        })
        maps5.append(m)
    res5 = _run(build_l5(), maps5)

    out = np.empty((B, S, D), dtype=f32)
    for core in cores:
        b, r = divmod(core, RANKS)
        out[b, r * T:(r + 1) * T, :] = res5[core]["out_fm"].T
    return out
